# version 10
# baseline (speedup 1.0000x reference)
import numpy as np
from contextlib import ExitStack
import concourse.bass as bass
import concourse.mybir as mybir
from concourse.bass_utils import run_bass_kernel_spmd

F32 = mybir.dt.float32
BF16 = mybir.dt.bfloat16
I32 = mybir.dt.int32
AF = mybir.ActivationFunctionType
ALU = mybir.AluOpType
AX = mybir.AxisListType

D = 1024
SEG = 512
NSEG = 8
NSEQ = 2
H = 16
DH = 64
HID = 3072
WIN = 256
QSCALE = 1.0 / 8.0
DQ = 1.0 / 127.0
EPS = 1e-5

_NSEG_RUN = NSEG
_SUBLAYERS = (1, 2, 3, 4)
_AA_LIMIT = 99
_DEBUG_SHAPE = None
_DEBUG_EMIT = None


class EngQ:

    def __init__(self, name, sem, self_chain=False):
        self.name = name
        self.sem = sem
        self.n = 0
        self.q = []
        self.self_chain = self_chain

    def emit(self, fn):
        if self.self_chain:
            return self.mark(fn)
        self.q.append(fn)

    def mark(self, fn, inc=1):
        if self.self_chain and self.n > 0:
            sem0, tgt = self.sem, self.n
            self.q.append(lambda e, sem=sem0, val=tgt: e.wait_ge(sem, val))
        self.n += inc
        sem, k = self.sem, inc
        self.q.append(lambda e, fn=fn, sem=sem, k=k: fn(e).then_inc(sem, k))
        return self.n

    def emit_raw(self, fn):
        self.q.append(fn)

    def wait(self, other, val=None):
        if val is None:
            val = other.n
        if val <= 0:
            return
        sem = other.sem
        self.q.append(lambda e, sem=sem, val=val: e.wait_ge(sem, val))


class Prog:
    def __init__(self, nc, sems, mksem):
        self.nc = nc
        self.t = EngQ("tensor", sems["pe"])
        self.v = EngQ("vector", sems["dve"], self_chain=True)
        self.s = EngQ("scalar", sems["act"], self_chain=True)
        self.g = EngQ("gpsimd", sems["pool"], self_chain=True)
        self.y = EngQ("sync", sems["dma_w"])
        self.state = {}
        self._mksem = mksem
        self.dstream = {}

    def stream(self, key):
        if key not in self.dstream:
            self.dstream[key] = EngQ(f"dma_{key}", self._mksem(f"dma_{key}"))
        return self.dstream[key]

    def dma(self, q, stream, out, in_, **kw):
        eq = {"y": self.y, "g": self.g, "s": self.s}[q]
        ctr = self.stream(stream)
        if eq.self_chain and eq.n > 0:
            sem0, tgt = eq.sem, eq.n
            eq.q.append(lambda e, sem=sem0, val=tgt: e.wait_ge(sem, val))
        ctr.n += 16
        sem = ctr.sem
        eq.q.append(lambda e, out=out, in_=in_, sem=sem, kw=kw:
                    e.dma_start(out=out, in_=in_, **kw).then_inc(sem, 16))
        return ctr.n

    def barrier(self, consumers, producers):
        for c in consumers:
            for pr in producers:
                if pr.n > 0:
                    c.wait(pr)


def build_program():
    nc = bass.Bass("TRN2", target_bir_lowering=False)
    T = {}

    T["xin"] = nc.dram_tensor("x", [NSEQ, NSEG * SEG, D], F32, kind="ExternalInput").ap()
    T["wq_d"] = nc.dram_tensor("wq", [D, D], F32, kind="ExternalInput").ap()
    T["wk_d"] = nc.dram_tensor("wk", [D, D], F32, kind="ExternalInput").ap()
    T["wv_d"] = nc.dram_tensor("wv", [D, D], F32, kind="ExternalInput").ap()
    T["wo_d"] = nc.dram_tensor("wo", [D, D], F32, kind="ExternalInput").ap()
    T["betas_d"] = nc.dram_tensor("betas", [1, H, 1, DH], F32, kind="ExternalInput").ap()
    T["wqkv_d"] = nc.dram_tensor("wqkv", [D, 3 * D], F32, kind="ExternalInput").ap()
    T["wout_d"] = nc.dram_tensor("wout", [D, D], F32, kind="ExternalInput").ap()
    T["wg_d"] = nc.dram_tensor("wg", [HID, D], I32, kind="ExternalInput").ap()
    T["wu_d"] = nc.dram_tensor("wu", [HID, D], I32, kind="ExternalInput").ap()
    T["wd_d"] = nc.dram_tensor("wd", [D, HID], I32, kind="ExternalInput").ap()
    T["yout"] = nc.dram_tensor("y", [NSEQ, NSEG * SEG, D], F32, kind="ExternalOutput").ap()
    if _DEBUG_SHAPE is not None:
        T["dbg"] = nc.dram_tensor("dbg", list(_DEBUG_SHAPE), F32, kind="ExternalOutput").ap()

    T["s_wqa"] = nc.dram_tensor("s_wqa", [H, 8, 128, DH], BF16, kind="Internal").ap()
    T["s_wka"] = nc.dram_tensor("s_wka", [H, 8, 128, DH], BF16, kind="Internal").ap()
    T["s_wva"] = nc.dram_tensor("s_wva", [H, 8, 128, DH], BF16, kind="Internal").ap()
    T["s_wo"] = nc.dram_tensor("s_wo", [16, 64, D], BF16, kind="Internal").ap()
    T["s_wqkv"] = nc.dram_tensor("s_wqkv", [24, 8, 128, 128], BF16, kind="Internal").ap()
    T["s_wout"] = nc.dram_tensor("s_wout", [8, 128, D], BF16, kind="Internal").ap()
    T["s_wg"] = nc.dram_tensor("s_wg", [24, 8, 128, 128], BF16, kind="Internal").ap()
    T["s_wu"] = nc.dram_tensor("s_wu", [24, 8, 128, 128], BF16, kind="Internal").ap()
    T["s_wd"] = nc.dram_tensor("s_wd", [24, 8, 128, 128], BF16, kind="Internal").ap()
    T["s_zc"] = nc.dram_tensor("s_zc", [DH, H], F32, kind="Internal").ap()

    with ExitStack() as ctx:
        def sb(name, shape, dt):
            return ctx.enter_context(nc.sbuf_tensor(name, shape, dt)).ap()

        def psv(name, shape, dt=F32):
            return ctx.enter_context(nc.psum_tensor(name, shape, dt)).ap()

        T["X"] = sb("X", [128, NSEQ * 4 * D], F32)
        T["kwin"] = sb("kwin", [128, NSEQ * 8 * WIN], BF16)
        T["vwin"] = sb("vwin", [128, NSEQ * 8 * WIN], BF16)
        T["mem_sb"] = sb("mem_sb", [128, NSEQ * H * DH], F32)
        T["membf"] = sb("membf", [128, DH], BF16)
        T["zc_sb"] = sb("zc_sb", [128, NSEQ * H], F32)
        T["invg"] = sb("invg", [128, H], F32)
        T["grow"] = sb("grow", [1, D], F32)
        T["ident"] = sb("ident", [128, 128], BF16)
        T["ident64"] = sb("ident64", [128, 64], BF16)
        T["ident64a"] = sb("ident64a", [128, 64], BF16)
        T["ident64f"] = sb("ident64f", [128, 64], F32)
        T["onesc"] = sb("onesc", [128, 64], BF16)
        T["onesf"] = sb("onesf", [1, 1], F32)
        T["mask_std"] = sb("mask_std", [128, 384], BF16)
        T["mask_f0"] = sb("mask_f0", [128, 384], BF16)
        T["mask_f1"] = sb("mask_f1", [128, 384], BF16)
        T["c12col"] = sb("c12col", [128, 1], F32)
        T["lncols"] = sb("lncols", [128, 32], F32)
        T["epscol"] = sb("epscol", [128, 1], F32)
        T["cols"] = sb("cols", [128, 32], F32)
        T["rows"] = sb("rows", [128, 1536], F32)
        T["rows_bf"] = sb("rows_bf", [1, 1024], BF16)
        T["TSB"] = sb("TSB", [128, 256], F32)
        T["ESC"] = sb("ESC", [128, 2048], BF16)
        T["BSB"] = sb("BSB", [128, 2 * 256], F32)
        T["SSB"] = sb("SSB", [128, 2 * 512], BF16)

        T["wbuf_a"] = sb("wbuf_a", [128, 2 * 3 * 512], BF16)
        T["wo_buf"] = sb("wo_buf", [128, 2 * D], BF16)
        T["wbuf_1"] = sb("wbuf_1", [128, 3 * 2048], BF16)
        T["wbuf_2"] = sb("wbuf_2", [128, 3 * 1024], BF16)

        AR = 99328
        arena = ctx.enter_context(nc.sbuf_tensor("arena", [128, AR], mybir.dt.uint8)).ap()

        def carve(off, shape, dt):
            nbytes = int(np.prod(shape[1:])) * mybir.dt.size(dt)
            assert off + nbytes <= AR, (off, nbytes, AR)
            return arena[:, off:off + nbytes].bitcast(dt)

        T["xlt"] = carve(0, [128, 2 * D], BF16)
        T["xlf"] = carve(4096, [128, 8 * SEG], BF16)
        T["QA"] = carve(12288, [128, H * SEG], BF16)
        T["KA"] = carve(28672, [128, H * SEG], BF16)
        T["VA"] = carve(45056, [128, H * SEG], BF16)
        T["SQA"] = carve(61440, [128, H * SEG], BF16)
        T["A2"] = carve(77824, [128, 8 * SEG], BF16)
        T["PB"] = carve(86016, [128, 8 * SEG], BF16)
        T["PT"] = carve(94208, [128, 4 * SEG], BF16)
        T["VTST"] = carve(98304, [128, 512], BF16)
        T["xlf2"] = carve(12288, [128, 8 * SEG], BF16)
        T["HA"] = carve(20480, [128, 24 * SEG], BF16)
        T["HB"] = carve(45056, [128, 24 * SEG], BF16)
        T["QKVB"] = carve(12288, [128, 24 * SEG], BF16)
        T["ABT"] = carve(36864, [128, 8 * SEG], BF16)
        T["PB2"] = carve(45056, [128, 8 * 384], BF16)
        T["PT2"] = carve(51200, [128, 4 * 384], BF16)
        T["VT2"] = carve(54272, [128, 2 * 6 * DH], BF16)
        T["cv_i"] = carve(12288, [128, 3 * D], I32)
        T["cv_f"] = carve(24576, [128, 3 * D], F32)
        T["cv_b"] = carve(36864, [128, 3 * D], BF16)
        T["cv_t"] = carve(43008, [128, 1024], BF16)

        P8 = psv("P8", [128, 4096], F32)
        T["P8"] = P8
        T["BK"] = [P8[:, i * 512:(i + 1) * 512] for i in range(8)]

        sems = {}
        for name in ["pe", "dve", "act", "pool", "dma_w"]:
            sems[name] = ctx.enter_context(nc.semaphore(name))
        mksem = lambda nm: ctx.enter_context(nc.semaphore(nm))
        p = Prog(nc, sems, mksem)

        build_phases(nc, p, T)

        with nc.Block() as block:
            @block.tensor
            def _(e):
                for f in p.t.q:
                    f(e)

            @block.vector
            def _(e):
                for f in p.v.q:
                    f(e)

            @block.scalar
            def _(e):
                for f in p.s.q:
                    f(e)

            @block.gpsimd
            def _(e):
                for f in p.g.q:
                    f(e)

            @block.sync
            def _(e):
                for f in p.y.q:
                    f(e)

    return nc



def build_phases(nc, p, T):
    prologue(nc, p, T)
    for n in range(_NSEG_RUN):
        if 1 in _SUBLAYERS:
            for s in range(NSEQ):
                ln_into_xlf(nc, p, T, seq=s, xlf=T["xlf"])
                attn_a(nc, p, T, seq=s, n=n)
        if 2 in _SUBLAYERS:
            ffn(nc, p, T)
        if 3 in _SUBLAYERS:
            for s in range(NSEQ):
                ln_into_xlf(nc, p, T, seq=s, xlf=T["xlf"])
                attn_b(nc, p, T, seq=s, n=n)
        if 4 in _SUBLAYERS:
            ffn(nc, p, T)
        for s in range(NSEQ):
            store_segment(nc, p, T, seq=s, n=n)
    if _DEBUG_EMIT is not None:
        _DEBUG_EMIT(nc, p, T)
    epilogue(nc, p, T)


def prologue(nc, p, T):
    BK = T["BK"]
    ident, ident64, ident64a, onesc = T["ident"], T["ident64"], T["ident64a"], T["onesc"]

    def asel(out, in_, cmp, fill, base, pattern, chm):
        p.g.emit(lambda g, out=out, in_=in_, cmp=cmp, fill=fill, base=base,
                 pattern=pattern, chm=chm: g.affine_select(
                     out=out, in_=in_, compare_op=cmp, fill=fill, base=base,
                     pattern=pattern, channel_multiplier=chm))

    def gmemset(ap_, val):
        p.g.emit(lambda g, ap_=ap_, val=val: g.memset(ap_, val))

    gmemset(ident, 0.0)
    asel(ident, ident, ALU.not_equal, 1.0, 0, [[-1, 128]], 1)
    gmemset(ident64, 0.0)
    asel(ident64, ident64, ALU.not_equal, 1.0, -64, [[-1, 64]], 1)
    gmemset(ident64a, 0.0)
    asel(ident64a, ident64a, ALU.not_equal, 1.0, 0, [[-1, 64]], 1)
    gmemset(T["ident64f"], 0.0)
    asel(T["ident64f"], T["ident64f"], ALU.not_equal, 1.0, -64, [[-1, 64]], 1)
    gmemset(onesc, 1.0)
    for m in (T["mask_std"], T["mask_f0"], T["mask_f1"]):
        gmemset(m, -12.0)
        asel(m, m, ALU.is_ge, -30012.0, -1, [[1, 384]], -1)
        asel(m, m, ALU.is_ge, -30012.0, 256, [[-1, 384]], 1)
    asel(T["mask_f0"], T["mask_f0"], ALU.is_ge, -30012.0, -256, [[1, 384]], 0)
    asel(T["mask_f1"], T["mask_f1"], ALU.is_ge, -30012.0, -128, [[1, 384]], 0)
    gmemset(T["c12col"], -12.0)
    gmemset(T["epscol"], EPS)
    gmemset(T["mem_sb"], 0.0)
    gmemset(T["zc_sb"], 0.0)
    gmemset(T["kwin"], 0.0)
    gmemset(T["vwin"], 0.0)
    gmemset(T["onesf"], 1.0)

    for s in range(NSEQ):
        dst = T["X"][:, s * 4096:(s + 1) * 4096].rearrange("p (lc d) -> p lc d", lc=4)
        src = T["xin"][s, 0:SEG, :].rearrange("(lc q) d -> q lc d", q=128)
        ld = p.dma("g", ("x", s), dst, src)
        p.state[("X", s)] = (p.stream(("x", s)), ld)

    p.dma("s", "misc", T["rows"][0:1, 0:1024], T["betas_d"].rearrange("a h b d -> a (h b d)"))
    p.s.wait(p.stream("misc"))
    p.s.mark(lambda s: s.activation(out=T["grow"], in_=T["rows"][0:1, 0:1024],
                                    func=AF.Sigmoid))
    p.t.wait(p.s)
    p.t.wait(p.g)

    def t_invg(t):
        last = None
        for hp in range(2):
            for h in range(H):
                last = t.matmul(BK[0][hp * 64:hp * 64 + 64, h:h + 1],
                                T["grow"][0:1, h * 64:(h + 1) * 64], T["onesf"],
                                start=True, stop=True,
                                tile_position=(0, hp * 64))
        return last

    p.t.mark(t_invg)
    p.v.wait(p.t)
    p.v.mark(lambda v: v.tensor_scalar(out=T["invg"], in0=BK[0][:, 0:H],
                                       scalar1=-1.0, scalar2=1.0,
                                       op0=ALU.mult, op1=ALU.add))
    bank_free = p.v.n

    cv_f, cv_b, cv_i, cv_t = T["cv_f"], T["cv_b"], T["cv_i"], T["cv_t"]
    halfF = 1536
    halfB = 1536

    conv_marks = []

    def conv_f32(src_rows, n_rows_chunks, width, store_fn, qscale_cols=None):
        for k in range(n_rows_chunks):
            sl = (k % 2) if width <= halfF else 0
            fa = cv_f[:, sl * halfF:sl * halfF + width]
            ba = cv_b[:, sl * halfB:sl * halfB + width]
            p.y.wait(p.stream(("cvS", sl)))
            p.y.wait(p.v)
            p.y.wait(p.s)
            if width > halfF:
                p.y.wait(p.stream(("cvS", 1 - sl)))
            ld = p.dma("y", ("cvL", sl), fa, src_rows(k))
            p.v.wait(p.stream(("cvL", sl)), ld)
            if qscale_cols is None:
                p.v.mark(lambda v, ba=ba, fa=fa: v.tensor_copy(out=ba, in_=fa))
            else:
                c0, c1 = qscale_cols
                if c0 > 0:
                    p.v.emit(lambda v, o=ba[:, 0:c0], i=fa[:, 0:c0]: v.tensor_copy(out=o, in_=i))
                p.v.emit(lambda v, o=ba[:, c0:c1], i=fa[:, c0:c1]: v.tensor_scalar(
                    out=o, in0=i, scalar1=QSCALE, scalar2=None, op0=ALU.mult))
                if c1 < width:
                    p.v.mark(lambda v, o=ba[:, c1:width], i=fa[:, c1:width]: v.tensor_copy(out=o, in_=i))
                else:
                    p.v.mark(lambda v: v.engine_nop())
            store_fn(k, sl, ba, p.v.n)

    for w_d, s_w, qs in ((T["wq_d"], T["s_wqa"], (0, D)), (T["wk_d"], T["s_wka"], None),
                         (T["wv_d"], T["s_wva"], None)):
        def store(k, sl, ba, vmark, s_w=s_w):
            out = s_w[:, k, :, :].transpose([1, 0, 2])
            in_ = ba.rearrange("p (g j) -> p g j", g=16)
            p.s.wait(p.v, vmark)
            p.dma("s", ("cvS", sl), out, in_)

        conv_f32(lambda k, w_d=w_d: w_d[k * 128:(k + 1) * 128, :], 8, D, store,
                 qscale_cols=qs)

    def store_wo(k, sl, ba, vmark):
        p.s.wait(p.v, vmark)
        p.dma("s", ("cvS", sl), T["s_wo"][2 * k], ba[0:64, :])
        p.dma("s", ("cvS", sl), T["s_wo"][2 * k + 1], ba[64:128, :])

    conv_f32(lambda k: T["wo_d"][k * 128:(k + 1) * 128, :], 8, D, store_wo)
    p.state["scr_a"] = [(c, c.n) for c in p.dstream.values()] + [(p.s, p.s.n), (p.g, p.g.n)]

    def store_wqkv(k, sl, ba, vmark):
        out = T["s_wqkv"][:, k, :, :].transpose([1, 0, 2])
        in_ = ba.rearrange("p (c j) -> p c j", c=24)
        p.s.wait(p.v, vmark)
        p.dma("s", ("cvS", sl), out, in_)

    conv_f32(lambda k: T["wqkv_d"][k * 128:(k + 1) * 128, :], 8, 3 * D, store_wqkv,
             qscale_cols=(0, D))

    def store_wout(k, sl, ba, vmark):
        p.s.wait(p.v, vmark)
        p.dma("s", ("cvS", sl), T["s_wout"][k], ba)

    conv_f32(lambda k: T["wout_d"][k * 128:(k + 1) * 128, :], 8, D, store_wout)
    p.state["scr_b"] = [(c, c.n) for c in p.dstream.values()] + [(p.s, p.s.n), (p.g, p.g.n)]

    p.t.wait(p.v, bank_free)

    def conv_int(src_rows, n_chunks, width, store_fn):
        for k in range(n_chunks):
            sl = (k % 2) if width <= halfF else 0
            ia = cv_i[:, sl * halfF:sl * halfF + width]
            fa = cv_f[:, sl * halfF:sl * halfF + width]
            ba = cv_b[:, sl * halfB:sl * halfB + width]
            p.y.wait(p.t)
            p.y.wait(p.v)
            p.y.wait(p.s)
            if width > halfF:
                p.y.wait(p.stream(("cvS", 1 - sl)))
            p.y.wait(p.stream(("cvS", sl)))
            ld = p.dma("y", ("cvL", sl), ia, src_rows(k))
            p.v.wait(p.stream(("cvL", sl)), ld)
            p.v.mark(lambda v, fa=fa, ia=ia: v.tensor_copy(out=fa, in_=ia))
            p.s.wait(p.v)
            p.s.mark(lambda s, ba=ba, fa=fa: s.activation(out=ba, in_=fa,
                                                          func=AF.Copy, scale=DQ))
            store_fn(k, ba, p.s.n)

    for w_d, s_w in ((T["wg_d"], T["s_wg"]), (T["wu_d"], T["s_wu"])):
        def store_g(r, ba, smark, s_w=s_w):
            p.t.wait(p.s, smark)
            p.t.wait(p.stream("cvt"))

            def t_fn(t, ba=ba):
                last = None
                for k in range(8):
                    last = t.transpose(BK[0].bitcast(BF16)[:, k * 128:(k + 1) * 128],
                                       ba[:, k * 128:(k + 1) * 128], T["ident"])
                return last

            p.t.mark(t_fn)
            p.v.wait(p.t)
            p.v.mark(lambda v: v.tensor_copy(out=cv_t, in_=BK[0].bitcast(BF16)[:, 0:1024]))
            p.g.wait(p.v)
            out = s_w[r].transpose([1, 0, 2])
            p.dma("g", "cvt", out, cv_t.rearrange("p (k j) -> p k j", k=8))

        conv_int(lambda r, w_d=w_d: w_d[r * 128:(r + 1) * 128, :], 24, D, store_g)

    def store_wd(r, ba, smark):
        p.t.wait(p.s, smark)
        for half in range(3):
            p.t.wait(p.stream("cvt"))

            def t_fn(t, ba=ba, half=half):
                last = None
                for kk in range(8):
                    c = half * 8 + kk
                    last = t.transpose(BK[0].bitcast(BF16)[:, kk * 128:(kk + 1) * 128],
                                       ba[:, c * 128:(c + 1) * 128], T["ident"])
                return last

            p.t.mark(t_fn)
            p.v.wait(p.t)
            p.v.mark(lambda v: v.tensor_copy(out=cv_t, in_=BK[0].bitcast(BF16)[:, 0:1024]))
            p.g.wait(p.v)
            for kk in range(8):
                c = half * 8 + kk
                p.dma("g", "cvt", T["s_wd"][c, r], cv_t.rearrange("p (k j) -> p k j", k=8)[:, kk, :])

    conv_int(lambda r: T["wd_d"][r * 128:(r + 1) * 128, :], 8, HID, store_wd)

    p.state["scratch"] = ([(c, c.n) for c in p.dstream.values()] +
                          [(p.g, p.g.n), (p.v, p.v.n), (p.s, p.s.n), (p.t, p.t.n)])


def wait_scratch(p, eng, key="scratch"):
    for q, val in p.state[key]:
        eng.wait(q, val)


def ln_into_xlf(nc, p, T, seq, xlf):
    X, xlt, lnc, BK = T["X"], T["xlt"], T["lncols"], T["BK"]
    xq = p.state.get(("X", seq), (None, 0))
    if xq[0] is not None:
        p.v.wait(xq[0], xq[1])
        p.s.wait(xq[0], xq[1])
    p.s.wait(p.t)
    X4 = X[:, seq * 4096:(seq + 1) * 4096]
    p.v.mark(lambda v: v.reduce_sum(out=lnc[:, 0:4],
                                    in_=X4.rearrange("p (lc d) -> p lc d", lc=4),
                                    axis=AX.X))
    for lc in range(4):
        Xt = X4[:, lc * D:(lc + 1) * D]
        xs = xlt[:, (lc % 2) * D:(lc % 2 + 1) * D]
        p.s.mark(lambda s, Xt=Xt, xs=xs, c=lnc[:, 4 + lc:5 + lc]: s.activation(
            out=xs, in_=Xt, func=AF.Square, accum_out=c))
    p.v.wait(p.s)
    p.v.emit(lambda v: v.tensor_scalar(out=lnc[:, 8:12], in0=lnc[:, 0:4],
                                       scalar1=1.0 / D, scalar2=None, op0=ALU.mult))
    p.v.emit(lambda v: v.tensor_mul(out=lnc[:, 12:16], in0=lnc[:, 8:12], in1=lnc[:, 8:12]))
    p.v.mark(lambda v: v.scalar_tensor_tensor(out=lnc[:, 16:20], in0=lnc[:, 4:8],
                                              scalar=1.0 / D, in1=lnc[:, 12:16],
                                              op0=ALU.mult, op1=ALU.subtract))
    p.s.wait(p.v)
    p.s.mark(lambda s: s.activation(out=lnc[:, 20:24], in_=lnc[:, 16:20],
                                    func=AF.Sqrt, bias=T["epscol"]))
    p.v.wait(p.s)
    p.v.emit(lambda v: v.reciprocal(out=lnc[:, 24:28], in_=lnc[:, 20:24]))
    p.v.mark(lambda v: v.scalar_tensor_tensor(out=lnc[:, 28:32], in0=lnc[:, 8:12],
                                              scalar=-1.0, in1=lnc[:, 24:28],
                                              op0=ALU.mult, op1=ALU.mult))
    for lc in range(4):
        Xt = X4[:, lc * D:(lc + 1) * D]
        xs = xlt[:, (lc % 2) * D:(lc % 2 + 1) * D]
        if lc >= 2:
            p.v.wait(p.t)
        p.v.mark(lambda v, Xt=Xt, xs=xs, c6=lnc[:, 24 + lc:25 + lc],
                 c7=lnc[:, 28 + lc:29 + lc]: v.tensor_scalar(
                     out=xs, in0=Xt, scalar1=c6, scalar2=c7,
                     op0=ALU.mult, op1=ALU.add))
        bank = BK[6 + (lc % 2)]
        p.t.wait(p.v)
        p.t.wait(p.v, p.state.get(("lnbank", 6 + lc % 2), 0))

        def t_fn(t, xs=xs, bank=bank):
            last = None
            for cc in range(8):
                last = t.transpose(bank.bitcast(BF16)[:, cc * 128:(cc + 1) * 128],
                                   xs[:, cc * 128:(cc + 1) * 128], T["ident"])
            return last

        p.t.mark(t_fn)
        p.v.wait(p.t)
        out = xlf.rearrange("p (cc l) -> p cc l", cc=8)[:, :, lc * 128:(lc + 1) * 128]
        p.v.mark(lambda v, out=out, bank=bank: v.tensor_copy(
            out=out, in_=bank.bitcast(BF16)[:, 0:1024].rearrange("p (cc l) -> p cc l", cc=8)))
        p.state[("lnbank", 6 + lc % 2)] = p.v.n
    p.state["xlf"] = (p.v, p.v.n)


def attn_a(nc, p, T, seq, n):
    BK = T["BK"]
    QA, KA, VA, SQA, A2 = T["QA"], T["KA"], T["VA"], T["SQA"], T["A2"]
    PB, PT = T["PB"], T["PT"]
    rows, cols = T["rows"], T["cols"]
    wbuf, wo_buf = T["wbuf_a"], T["wo_buf"]
    wait_scratch(p, p.y)
    p.y.wait(p.t)
    p.t.wait(p.g)
    p.v.wait(p.t)
    p.s.wait(p.t)
    xq, xv = p.state["xlf"]
    p.t.wait(xq, xv)

    ev_marks = {}
    for g in range(16):
        sl = g % 2
        b0, b1 = BK[sl * 2], BK[sl * 2 + 1]
        if g >= 2:
            pe_done = p.state[("aproj_pe", sl)]
            p.y.wait(p.t, pe_done)
        for idx, s_w in enumerate((T["s_wqa"], T["s_wka"], T["s_wva"])):
            dst = wbuf[:, sl * 1536 + idx * 512: sl * 1536 + idx * 512 + 512]
            dst = dst.rearrange("p (k j) -> p k j", k=8)
            ld = p.dma("y", ("wA", sl), dst, s_w[g].transpose([1, 0, 2]))
        p.t.wait(p.stream(("wA", sl)), ld)
        if g >= 2:
            for q, val in ev_marks[sl]:
                p.t.wait(q, val)

        def t_fn(t, sl=sl, b0=b0, b1=b1):
            last = None
            for idx, (bank, base) in enumerate(((b0, 64), (b1, 64), (b0, 0))):
                for k in range(8):
                    last = t.matmul(
                        bank[base:base + 64, :],
                        T["wbuf_a"][:, sl * 1536 + idx * 512 + k * 64: sl * 1536 + idx * 512 + (k + 1) * 64],
                        T["xlf"][:, k * 512:(k + 1) * 512],
                        start=(k == 0), stop=(k == 7), tile_position=(0, base))
            return last

        p.t.mark(t_fn)
        p.state[("aproj_pe", sl)] = p.t.n
        sp = (g % 2) * 256 + (g // 2) * 32
        qdst = QA[64:128, :].rearrange("p (h s) -> p h s", h=H)[:, :, sp:sp + 32]
        kdst = KA[64:128, :].rearrange("p (h s) -> p h s", h=H)[:, :, sp:sp + 32]
        vdst = VA[0:64, :].rearrange("p (h s) -> p h s", h=H)[:, :, sp:sp + 32]
        qsrc = b0[64:128, :].rearrange("p (h i) -> p h i", h=H)
        ksrc = b1[64:128, :].rearrange("p (h i) -> p h i", h=H)
        vsrc = b0[0:64, :].rearrange("p (h i) -> p h i", h=H)
        p.v.wait(p.t)
        p.s.wait(p.t)
        p.v.mark(lambda v, o=qdst, i=qsrc: v.tensor_copy(out=o, in_=i))
        p.s.mark(lambda s, o=kdst, i=ksrc: s.copy(out=o, in_=i))
        p.v.mark(lambda v, o=vdst, i=vsrc: v.tensor_copy(out=o, in_=i))
        ev_marks[sl] = [(p.v, p.v.n), (p.s, p.s.n)]
    p.t.wait(p.v)
    p.t.wait(p.s)
    proj_done_v = p.v.n
    proj_done_s = p.s.n

    RF = VA[64:128, :]
    EH = T["ESC"][64:128, :]
    sh = seq * H
    zpart = T["rows"][64:128, 0:16]
    rsum = T["rows"][64:128, 16:32]
    zinc = T["rows"][64:128, 32:48]
    gz = T["rows"][0:1, 512:1536]

    def sqsk_quarter(q):
        blk = slice(q * 2048, (q + 1) * 2048)
        rfq = RF[:, blk]
        ehq = EH
        p.s.wait(p.v)
        p.s.mark(lambda s, rfq=rfq: s.activation(out=rfq, in_=QA[64:128, blk], func=AF.Relu))
        p.v.wait(p.s)
        p.v.mark(lambda v: v.tensor_sub(out=SQA[64:128, blk], in0=QA[64:128, blk], in1=rfq))
        p.s.wait(p.v)
        p.s.mark(lambda s: s.activation(out=ehq, in_=SQA[64:128, blk], func=AF.Exp,
                                        scale=1.0 / QSCALE))
        p.v.wait(p.s)
        p.v.mark(lambda v: v.scalar_tensor_tensor(
            out=SQA[64:128, blk], in0=rfq, scalar=1.0 / QSCALE, in1=ehq,
            op0=ALU.mult, op1=ALU.add))
        p.state[("sqq", q)] = p.v.n
        p.s.wait(p.v)
        p.s.mark(lambda s, rfq=rfq: s.activation(out=rfq, in_=KA[64:128, blk], func=AF.Relu))
        p.v.wait(p.s)
        p.v.emit(lambda v, o=rsum[:, q * 4:(q + 1) * 4], rfq=rfq: v.reduce_sum(
            out=o, in_=rfq.rearrange("p (h l) -> p h l", h=4), axis=AX.X))
        p.v.mark(lambda v: v.tensor_sub(out=ehq, in0=KA[64:128, blk], in1=rfq))
        p.s.wait(p.v)
        p.s.mark(lambda s, rfq=rfq: s.activation(out=rfq, in_=ehq, func=AF.Exp))
        p.v.wait(p.s)
        p.v.emit(lambda v, o=zpart[:, q * 4:(q + 1) * 4], rfq=rfq: v.reduce_sum(
            out=o, in_=rfq.rearrange("p (h l) -> p h l", h=4), axis=AX.X))
        p.v.emit(lambda v, o=zinc[:, q * 4:(q + 1) * 4]: v.scalar_tensor_tensor(
            out=o, in0=rsum[:, q * 4:(q + 1) * 4], scalar=1.0,
            in1=zpart[:, q * 4:(q + 1) * 4], op0=ALU.mult, op1=ALU.add))
        zcs = T["zc_sb"][64:128, sh + q * 4:sh + (q + 1) * 4]
        p.v.mark(lambda v, zcs=zcs: v.tensor_add(out=zcs, in0=zcs,
                                                 in1=zinc[:, q * 4:(q + 1) * 4]))
        zq_ready = p.v.n
        p.y.wait(p.v, zq_ready)
        st1 = p.dma("y", "zc", T["s_zc"][:, q * 4:(q + 1) * 4],
                    T["zc_sb"][64:128, sh + q * 4:sh + (q + 1) * 4])
        p.y.wait(p.stream("zc"), st1)
        gq = slice(q * 256, (q + 1) * 256)
        st2 = p.dma("y", "zc", rows[0:1, 512 + q * 256:512 + (q + 1) * 256],
                    T["s_zc"][:, q * 4:(q + 1) * 4].transpose([1, 0]),
                    allow_slow_non_contiguous=True)
        p.v.wait(p.stream("zc"), st2)
        p.v.emit(lambda v, gq=gq: v.reciprocal(out=gz[:, gq], in_=gz[:, gq]))
        p.v.mark(lambda v, gq=gq: v.tensor_mul(out=gz[:, gq], in0=T["grow"][:, gq],
                                               in1=gz[:, gq]))
        p.state[("gzq", q)] = p.v.n

    AV_BANK, NUM_BANK = 5, 6
    PTB = T["P8"][:, 2 * 512:4 * 512].bitcast(BF16)
    TRB = T["P8"][:, 4 * 512:5 * 512].bitcast(BF16)
    B7 = BK[7]
    VT = T["VTST"][:, 0:256]
    ST = T["VTST"][:, 256:512]
    rows = T["rows"]
    cols = T["cols"]

    def small_pe_1(h):
        p.t.wait(p.v, p.state[("sqq", h // 4)])
        p.t.wait(p.v, p.state.get(("a7",), 0))
        p.t.mark(lambda t, i=SQA[64:128, h * 512:(h + 1) * 512]: t.matmul(
            B7[0:1, 0:512], T["onesc"][64:128, 0:1], i, start=True, stop=True,
            tile_position=(64, 0)))
        p.v.wait(p.t)
        p.v.mark(lambda v: v.reciprocal(out=rows[0:1, 0:512], in_=B7[0:1, 0:512]))
        p.state[("a7",)] = p.v.n

    def small_pe_2(h):
        p.t.wait(p.v, p.state[("a7",)])
        p.t.wait(p.v, p.state[("gzq", h // 4)])

        def t_b(t, h=h):
            last = None
            for pp in range(2):
                last = t.matmul(B7[pp * 64:pp * 64 + 64, 0:256],
                                rows[0:1, 512 + h * 64:512 + (h + 1) * 64],
                                rows[0:1, pp * 256:pp * 256 + 256],
                                start=True, stop=True, tile_position=(0, pp * 64))
            return last

        p.t.mark(t_b)
        p.v.wait(p.t)
        p.v.mark(lambda v, o=T["BSB"][:, (h % 2) * 256:(h % 2) * 256 + 256]:
                 v.tensor_copy(out=o, in_=B7[:, 0:256]))
        p.state[("a7",)] = p.v.n

    def scores_part(h):
        hof = h * 512
        cb = (h % 2) * 8
        for sc in range(4):
            sb_ = sc % 2
            p.t.wait(p.s, p.state.get(("aS_free", sb_), 0))
            p.t.mark(lambda t, b=BK[sb_], lh=QA[64:128, hof + sc * 128:hof + sc * 128 + 128],
                     rh=KA[64:128, hof:hof + 512]: t.matmul(b, lh, rh, start=True, stop=True))
            p.s.wait(p.t)
            p.s.wait(p.t, p.state.get(("aPBt", h % 2), 0))
            pb = PB[:, ((h % 2) * 4 + sc) * 512:((h % 2) * 4 + sc + 1) * 512]
            p.s.mark(lambda s, o=pb, b=BK[sb_], c2=cols[:, cb + sc:cb + sc + 1]:
                     s.activation(out=o, in_=b, func=AF.Exp, bias=T["c12col"], accum_out=c2))
            p.state[("aS_free", sb_)] = p.s.n
        p.v.wait(p.s)
        p.v.mark(lambda v: v.reciprocal(out=cols[:, cb + 4:cb + 8], in_=cols[:, cb:cb + 4]))
        for sc in range(4):
            pb = PB[:, ((h % 2) * 4 + sc) * 512:((h % 2) * 4 + sc + 1) * 512]
            p.v.mark(lambda v, o=pb, c3=cols[:, cb + 4 + sc:cb + 5 + sc]: v.tensor_scalar(
                out=o, in0=o, scalar1=c3, scalar2=None, op0=ALU.mult))
        p.state[("norms", h)] = p.v.n

    def tail_part(h):
        hof = h * 512
        p.t.wait(p.s, p.state.get(("aTR_free",), 0))

        def t_vt4(t, hof=hof):
            last = None
            for j in range(4):
                last = t.transpose(TRB[:, j * 64:(j + 1) * 64],
                                   VA[0:64, hof + j * 128:hof + (j + 1) * 128],
                                   T["ident64a"][0:64, :])
            return last

        def t_st4(t, hof=hof):
            last = None
            for j in range(4):
                last = t.transpose(TRB[:, 256 + j * 64: 256 + (j + 1) * 64],
                                   SQA[64:128, hof + j * 128:hof + (j + 1) * 128],
                                   T["ident64"][64:128, :])
            return last

        m1 = p.t.mark(t_vt4)
        p.t.wait(p.t, m1)
        p.t.mark(t_st4)
        p.s.wait(p.t)
        p.s.mark(lambda s: s.copy(out=T["VTST"], in_=TRB[:, 0:512]))
        vtst_ready = p.s.n
        p.state[("aTR_free",)] = p.s.n
        p.t.wait(p.s, vtst_ready)
        p.t.wait(p.v, p.state.get(("a7",), 0))

        def t_mem(t):
            last = None
            for j in range(4):
                last = t.matmul(B7[64:128, 256:320], ST[:, j * 64:(j + 1) * 64],
                                VT[:, j * 64:(j + 1) * 64],
                                start=(j == 0), stop=(j == 3), tile_position=(0, 64))
            return last

        p.t.mark(t_mem)
        p.v.wait(p.t)
        mm = T["mem_sb"][64:128, (sh + h) * 64:(sh + h + 1) * 64]
        p.v.mark(lambda v, mm=mm: v.tensor_add(out=mm, in0=mm, in1=B7[64:128, 256:320]))
        p.v.mark(lambda v, mm=mm: v.tensor_copy(out=T["membf"][64:128, :], in_=mm))
        mem_ready = p.v.n
        p.state[("a7",)] = p.v.n
        p.t.wait(p.v, p.state[("norms", h)])
        p.t.wait(p.v, p.state.get(("aAV_free",), 0))
        evac_marks = {}
        for sc in range(5):
            if sc < 4:
                tb = PTB[:, (sc % 2) * 1024:(sc % 2) * 1024 + 512]
                p.t.wait(p.s, p.state.get(("aPT_free", sc % 2), 0))

                def t_pt4(t, h=h, sc=sc, tb=tb):
                    last = None
                    for j in range(4):
                        last = t.transpose(tb[:, j * 128:(j + 1) * 128],
                                           PB[:, ((h % 2) * 4 + sc) * 512 + j * 128:
                                              ((h % 2) * 4 + sc) * 512 + (j + 1) * 128],
                                           T["ident"])
                    return last

                p.t.mark(t_pt4)
                p.s.wait(p.t)
                out = PT.rearrange("p (j s2) -> p j s2", j=4)[:, :, sc * 128:(sc + 1) * 128]
                p.s.mark(lambda s, out=out, tb=tb: s.copy(
                    out=out, in_=tb.rearrange("p (j c) -> p j c", j=4)))
                p.state[("aPT_free", sc % 2)] = p.s.n
                evac_marks[sc] = p.s.n
            if sc >= 1:
                sp_ = sc - 1
                pp = sp_ // 2
                p.t.wait(p.s, evac_marks[sp_])

                def t_av4(t, sp_=sp_, pp=pp):
                    last = None
                    for j in range(4):
                        last = t.matmul(BK[AV_BANK][pp * 64:pp * 64 + 64, sp_ * 128:(sp_ + 1) * 128],
                                        VT[:, j * 64:(j + 1) * 64],
                                        PT[:, j * 512 + sp_ * 128: j * 512 + (sp_ + 1) * 128],
                                        start=(j == 0), stop=(j == 3), tile_position=(0, pp * 64))
                    return last

                p.t.mark(t_av4)
        p.state[("aPBt", h % 2)] = p.t.n
        p.t.wait(p.v, mem_ready)
        p.t.wait(p.v, p.state.get(("aNUM_free",), 0))

        def t_num(t, hof=hof):
            last = None
            for sc in range(4):
                pp = sc // 2
                last = t.matmul(BK[NUM_BANK][pp * 64:pp * 64 + 64, sc * 128:(sc + 1) * 128],
                                T["membf"][64:128, :],
                                SQA[64:128, hof + sc * 128:hof + (sc + 1) * 128],
                                start=True, stop=True, tile_position=(64, pp * 64))
            return last

        p.t.mark(t_num)
        p.v.wait(p.t)
        for pp in range(2):
            tsb = T["TSB"][pp * 64:pp * 64 + 64, :]
            p.v.emit(lambda v, o=tsb, nu=BK[NUM_BANK][pp * 64:pp * 64 + 64, pp * 256:pp * 256 + 256],
                     bs=T["BSB"][pp * 64:pp * 64 + 64, (h % 2) * 256:(h % 2) * 256 + 256]:
                     v.scalar_tensor_tensor(out=o, in0=nu, scalar=1.0, in1=bs,
                                            op0=ALU.mult, op1=ALU.mult))
            a2o = A2[pp * 64:pp * 64 + 64, :].rearrange("p (cc l) -> p cc l", cc=8)[:, :, h * 32:(h + 1) * 32]
            p.v.mark(lambda v, o=a2o, av=BK[AV_BANK][pp * 64:pp * 64 + 64, pp * 256:pp * 256 + 256],
                     ig=T["invg"][pp * 64:pp * 64 + 64, h:h + 1], ts=tsb:
                     v.scalar_tensor_tensor(out=o, in0=av, scalar=ig, in1=ts,
                                            op0=ALU.mult, op1=ALU.add))
        p.state[("aAV_free",)] = p.v.n
        p.state[("aNUM_free",)] = p.v.n

    for h in range(H if _AA_LIMIT >= 1 else 0):
        if h % 4 == 0:
            sqsk_quarter(h // 4)
        small_pe_1(h)
        scores_part(h)
        if h >= 1:
            tail_part(h - 1)
        small_pe_2(h)
    if _AA_LIMIT >= 1 and H > 0:
        tail_part(H - 1)
    a2_ready = p.v.n

    for g in range(16 if _AA_LIMIT >= 7 else 0):
        sl = g % 2
        pp = g % 2
        cc = g // 2
        if g >= 2:
            p.y.wait(p.t, p.state[("awo_pe", sl)])
        dst = wo_buf[pp * 64:pp * 64 + 64, sl * D:(sl + 1) * D]
        ld = p.dma("y", ("wo", sl), dst, T["s_wo"][g])
        p.t.wait(p.stream(("wo", sl)), ld)
        if g == 0:
            p.t.wait(p.v, a2_ready)

        def t_wo(t, g=g, sl=sl, pp=pp, cc=cc):
            last = None
            for lc in range(4):
                for dhf in range(2):
                    last = t.matmul(BK[lc * 2 + dhf],
                                    A2[pp * 64:pp * 64 + 64, cc * 512 + lc * 128: cc * 512 + (lc + 1) * 128],
                                    wo_buf[pp * 64:pp * 64 + 64, sl * D + dhf * 512: sl * D + (dhf + 1) * 512],
                                    start=(g == 0), stop=(g == 15), tile_position=(pp * 64, 0))
            return last

        p.t.mark(t_wo)
        p.state[("awo_pe", sl)] = p.t.n
    if _AA_LIMIT >= 7:
        p.v.wait(p.t)
        for lc in range(4):
            xs = T["X"][:, seq * 4096 + lc * D: seq * 4096 + (lc + 1) * D]
            b = T["P8"][:, lc * 1024:(lc + 1) * 1024]
            p.v.mark(lambda v, xs=xs, b=b: v.tensor_add(out=xs, in0=xs, in1=b))
        p.state[("X", seq)] = (p.v, p.v.n)
    p.state["xlf_free"] = [(p.t, p.t.n)]
    p.state["banks_free"] = (p.v, p.v.n)


def ffn(nc, p, T):
    BK = T["BK"]
    wait_scratch(p, p.y)
    p.y.wait(p.t)
    p.t.wait(p.g)
    p.v.wait(p.g)
    ln_into_xlf(nc, p, T, seq=0, xlf=T["xlf"])
    xlf_a_mark = p.state["xlf"]
    ln_into_xlf(nc, p, T, seq=1, xlf=T["xlf2"])
    xlf_b_mark = p.state["xlf"]
    p.t.wait(*xlf_a_mark)
    p.t.wait(*xlf_b_mark)
    q, val = p.state.get("banks_free", (None, 0))
    if q is not None:
        p.t.wait(q, val)
    xls = (T["xlf"], T["xlf2"])
    hts = (T["HA"], T["HB"])
    for c in range(24):
        sl = c % 3
        if c >= 3:
            p.y.wait(p.t, p.state[("f1_pe", sl)])
        for idx, s_w in enumerate((T["s_wg"], T["s_wu"])):
            dst = T["wbuf_1"][:, sl * 2048 + idx * 1024: sl * 2048 + (idx + 1) * 1024]
            ld = p.dma("y", ("f1", sl), dst.rearrange("p (k j) -> p k j", k=8),
                       s_w[c].transpose([1, 0, 2]))
        p.t.wait(p.stream(("f1", sl)), ld)
        for sq in range(NSEQ):
            bG = BK[(sl * 2 + sq) % 4]
            bU = BK[4 + (sl * 2 + sq) % 4]
            key = ("f1_banks", (sl * 2 + sq) % 4)
            qv = p.state.get(key, 0)
            p.t.wait(p.v, qv)

            def t_gu(t, sl=sl, sq=sq, bG=bG, bU=bU):
                last = None
                for k in range(8):
                    last = t.matmul(bG, T["wbuf_1"][:, sl * 2048 + k * 128:(sl * 2048 + (k + 1) * 128)],
                                    xls[sq][:, k * 512:(k + 1) * 512],
                                    start=(k == 0), stop=(k == 7))
                for k in range(8):
                    last = t.matmul(bU, T["wbuf_1"][:, sl * 2048 + 1024 + k * 128: sl * 2048 + 1024 + (k + 1) * 128],
                                    xls[sq][:, k * 512:(k + 1) * 512],
                                    start=(k == 0), stop=(k == 7))
                return last

            p.t.mark(t_gu)
            p.state[("f1_pe", sl)] = p.t.n
            ssl = (c * 2 + sq) % 2
            ssb = T["SSB"][:, ssl * 512:(ssl + 1) * 512]
            p.s.wait(p.t)
            p.s.wait(p.v, p.state.get(("ssb_free", ssl), 0))
            p.s.mark(lambda s, bG=bG, ssb=ssb: s.activation(out=ssb, in_=bG, func=AF.Silu))
            p.v.wait(p.s)
            hdst = hts[sq][:, c * 512:(c + 1) * 512]
            p.v.mark(lambda v, o=hdst, bU=bU, ssb=ssb: v.tensor_mul(out=o, in0=ssb, in1=bU))
            p.state[key] = p.v.n
            p.state[("ssb_free", ssl)] = p.v.n
    h_ready = p.v.n
    for sq in range(NSEQ):
        for c in range(24):
            sl = c % 3
            if c >= 3 or sq > 0:
                key = ("f2_pe", sl)
                if key in p.state:
                    p.y.wait(p.t, p.state[key])
            dst = T["wbuf_2"][:, sl * 1024:(sl + 1) * 1024]
            ld = p.dma("y", ("f2", sl), dst.rearrange("p (k j) -> p k j", k=8),
                       T["s_wd"][c].transpose([1, 0, 2]))
            p.t.wait(p.stream(("f2", sl)), ld)
            if c == 0:
                p.t.wait(p.v, h_ready if sq == 0 else p.state[("f2_res", 0)])

            def t_dn(t, sq=sq, c=c, sl=sl):
                last = None
                for lc in range(4):
                    for dhf in range(2):
                        last = t.matmul(BK[lc * 2 + dhf],
                                        hts[sq][:, c * 512 + lc * 128: c * 512 + (lc + 1) * 128],
                                        T["wbuf_2"][:, sl * 1024 + dhf * 512: sl * 1024 + (dhf + 1) * 512],
                                        start=(c == 0), stop=(c == 23))
                return last

            p.t.mark(t_dn)
            p.state[("f2_pe", sl)] = p.t.n
        p.v.wait(p.t)
        for lc in range(4):
            xs = T["X"][:, sq * 4096 + lc * D: sq * 4096 + (lc + 1) * D]
            b = T["P8"][:, lc * 1024:(lc + 1) * 1024]
            p.v.mark(lambda v, xs=xs, b=b: v.tensor_add(out=xs, in0=xs, in1=b))
        p.state[("f2_res", sq)] = p.v.n
        p.state[("X", sq)] = (p.v, p.v.n)
    p.state["xlf_free"] = [(p.t, p.t.n)]
    p.state["banks_free"] = (p.v, p.v.n)


def attn_b(nc, p, T, seq, n):
    BK = T["BK"]
    QKVB, ABT, PB2, PT2, VT2 = T["QKVB"], T["ABT"], T["PB2"], T["PT2"], T["VT2"]
    cols = T["cols"]
    wait_scratch(p, p.y)
    p.y.wait(p.t)
    p.t.wait(p.s)
    p.v.wait(p.s)
    xq, xv = p.state["xlf"]
    p.t.wait(xq, xv)
    q, val = p.state.get("banks_free", (None, 0))
    if q is not None:
        p.t.wait(q, val)
    for c in range(24):
        sl = c % 3
        if c >= 3:
            p.y.wait(p.t, p.state[("bproj_pe", sl)])
        dst = T["wbuf_1"][:, sl * 2048: sl * 2048 + 1024]
        ld = p.dma("y", ("qkv", sl), dst.rearrange("p (k j) -> p k j", k=8),
                   T["s_wqkv"][c].transpose([1, 0, 2]))
        p.t.wait(p.stream(("qkv", sl)), ld)
        bank = BK[c % 2]
        qv = p.state.get(("bproj_banks", c % 2), 0)
        p.t.wait(p.v, qv)
        p.t.wait(p.s, p.state.get(("bproj_banks_s", c % 2), 0))

        def t_p(t, sl=sl, bank=bank):
            last = None
            for k in range(8):
                last = t.matmul(bank, T["wbuf_1"][:, sl * 2048 + k * 128: sl * 2048 + (k + 1) * 128],
                                T["xlf"][:, k * 512:(k + 1) * 512],
                                start=(k == 0), stop=(k == 7))
            return last

        p.t.mark(t_p)
        p.state[("bproj_pe", sl)] = p.t.n
        dstq = QKVB[:, c * 512:(c + 1) * 512]
        if c % 2 == 0:
            p.v.wait(p.t)
            p.v.mark(lambda v, o=dstq, b=bank: v.tensor_copy(out=o, in_=b))
            p.state[("bproj_banks", c % 2)] = p.v.n
        else:
            p.s.wait(p.t)
            p.s.mark(lambda s, o=dstq, b=bank: s.copy(out=o, in_=b))
            p.state[("bproj_banks_s", c % 2)] = p.s.n
    p.t.wait(p.v)
    p.t.wait(p.s)
    qkv_ready_v, qkv_ready_s = p.v.n, p.s.n

    S_BANKS = (0, 1, 6)
    AV_BANK = 5
    PTB2 = T["P8"][:, 2 * 512:4 * 512].bitcast(BF16)
    TRB2 = T["P8"][:, 4 * 512:5 * 512].bitcast(BF16)

    def scores_b(h):
        b = (h % 2) * 64
        cq, ck, cv = h // 2, 8 + h // 2, 16 + h // 2
        idt = T["ident64a"][0:64, :] if b == 0 else T["ident64"][64:128, :]
        cb = 16 + (h % 2) * 8
        p.t.wait(p.s, p.state.get(("bTR_free",), 0))

        def t_vt(t, b=b, cv=cv, idt=idt):
            last = None
            for j in range(2):
                src = T["vwin"][b:b + 64, seq * 2048 + (cv - 16) * 256 + j * 128:
                                seq * 2048 + (cv - 16) * 256 + (j + 1) * 128]
                last = t.transpose(TRB2[:, j * 64:(j + 1) * 64], src, idt)
            for j in range(4):
                src = QKVB[b:b + 64, cv * 512 + j * 128: cv * 512 + (j + 1) * 128]
                last = t.transpose(TRB2[:, (2 + j) * 64:(3 + j) * 64], src, idt)
            return last

        p.t.mark(t_vt)
        p.s.wait(p.t)
        p.s.mark(lambda s, o=VT2[:, (h % 2) * 384:(h % 2) * 384 + 384]:
                 s.copy(out=o, in_=TRB2[:, 0:384]))
        p.state[("vt", h)] = p.s.n
        p.state[("bTR_free",)] = p.s.n
        for qc in range(4):
            sb_ = S_BANKS[(h * 4 + qc) % 3]
            lh = QKVB[b:b + 64, cq * 512 + qc * 128: cq * 512 + (qc + 1) * 128]
            pieces = []
            wbase = seq * 2048 + (ck - 8) * 256
            if qc == 0:
                pieces.append((0, 256, T["kwin"][b:b + 64, wbase:wbase + 256]))
                pieces.append((256, 384, QKVB[b:b + 64, ck * 512: ck * 512 + 128]))
            elif qc == 1:
                pieces.append((0, 128, T["kwin"][b:b + 64, wbase + 128:wbase + 256]))
                pieces.append((128, 384, QKVB[b:b + 64, ck * 512: ck * 512 + 256]))
            elif qc == 2:
                pieces.append((0, 384, QKVB[b:b + 64, ck * 512: ck * 512 + 384]))
            else:
                pieces.append((0, 384, QKVB[b:b + 64, ck * 512 + 128: ck * 512 + 512]))
            p.t.wait(p.s, p.state.get(("bS_free", sb_), 0))
            p.t.wait(p.v, p.state.get(("bS_freev", sb_), 0))

            def t_sc(t, lh=lh, pieces=pieces, sb_=sb_):
                last = None
                for (c0, c1, src) in pieces:
                    last = t.matmul(BK[sb_][:, c0:c1], lh, src, start=True, stop=True)
                return last

            p.t.mark(t_sc)
            if n == 0 and qc == 0:
                msk = T["mask_f0"]
            elif n == 0 and qc == 1:
                msk = T["mask_f1"]
            else:
                msk = T["mask_std"]
            p.v.wait(p.t)
            p.v.mark(lambda v, b_=BK[sb_], m=msk: v.tensor_add(out=b_[:, 0:384],
                                                               in0=b_[:, 0:384], in1=m))
            p.state[("bS_freev", sb_)] = p.v.n
            p.s.wait(p.v)
            p.s.wait(p.t, p.state.get(("bPBt", h % 2), 0))
            pb = PB2[:, ((h % 2) * 4 + qc) * 384:((h % 2) * 4 + qc + 1) * 384]
            c2 = cols[:, cb + qc:cb + qc + 1]
            p.s.mark(lambda s, o=pb, b_=BK[sb_], c2=c2: s.activation(
                out=o, in_=b_[:, 0:384], func=AF.Exp, accum_out=c2))
            p.state[("bS_free", sb_)] = p.s.n
        p.v.wait(p.s)
        p.v.mark(lambda v: v.reciprocal(out=cols[:, cb + 4:cb + 8], in_=cols[:, cb:cb + 4]))
        for qc in range(4):
            pb = PB2[:, ((h % 2) * 4 + qc) * 384:((h % 2) * 4 + qc + 1) * 384]
            p.v.mark(lambda v, o=pb, c3=cols[:, cb + 4 + qc:cb + 5 + qc]: v.tensor_scalar(
                out=o, in0=o, scalar1=c3, scalar2=None, op0=ALU.mult))
        p.state[("bnorms", h)] = p.v.n

    def tail_b(h):
        b = (h % 2) * 64
        cq = h // 2
        p.t.wait(p.v, p.state[("bnorms", h)])
        p.t.wait(p.s, p.state[("vt", h)])
        p.t.wait(p.s, p.state.get(("bAV_free",), 0))
        evac_marks = {}
        for qc in range(5):
            if qc < 4:
                tb = PTB2[:, (qc % 2) * 1024:(qc % 2) * 1024 + 384]
                p.t.wait(p.s, p.state.get(("bPT_free", qc % 2), 0))

                def t_pt3(t, h=h, qc=qc, tb=tb):
                    last = None
                    for j in range(3):
                        last = t.transpose(tb[:, j * 128:(j + 1) * 128],
                                           PB2[:, ((h % 2) * 4 + qc) * 384 + j * 128:
                                               ((h % 2) * 4 + qc) * 384 + (j + 1) * 128],
                                           T["ident"])
                    return last

                p.t.mark(t_pt3)
                p.s.wait(p.t)
                p.s.mark(lambda s, o=PT2[:, qc * 384:(qc + 1) * 384], tb=tb:
                         s.copy(out=o, in_=tb))
                p.state[("bPT_free", qc % 2)] = p.s.n
                evac_marks[qc] = p.s.n
            if qc >= 1:
                qp = qc - 1
                p.t.wait(p.s, evac_marks[qp])

                def t_av3(t, b=b, h=h, qp=qp):
                    last = None
                    for j in range(3):
                        kc = qp + j
                        last = t.matmul(BK[AV_BANK][b:b + 64, qp * 128:(qp + 1) * 128],
                                        VT2[:, (h % 2) * 384 + kc * 64:(h % 2) * 384 + (kc + 1) * 64],
                                        PT2[:, qp * 384 + j * 128: qp * 384 + (j + 1) * 128],
                                        start=(j == 0), stop=(j == 2), tile_position=(0, b))
                    return last

                p.t.mark(t_av3)
        p.state[("bPBt", h % 2)] = p.t.n
        p.s.wait(p.t)
        abdst = ABT[b:b + 64, :].rearrange("p (cc l) -> p cc l", cc=8)[:, cq:cq + 1, :]
        avsrc = BK[AV_BANK][b:b + 64, 0:512].rearrange("p (cc l) -> p cc l", cc=1)
        p.s.mark(lambda s, o=abdst, i=avsrc: s.copy(out=o, in_=i))
        p.state[("bAV_free",)] = p.s.n

    for h in range(H):
        scores_b(h)
        if h >= 1:
            tail_b(h - 1)
    tail_b(H - 1)
    ab_ready_s = p.s.n
    p.s.wait(p.t)
    for c in range(8):
        ksrc = QKVB[:, (8 + c) * 512 + 256:(8 + c) * 512 + 512]
        vsrc = QKVB[:, (16 + c) * 512 + 256:(16 + c) * 512 + 512]
        kdst = T["kwin"][:, seq * 2048 + c * 256: seq * 2048 + (c + 1) * 256]
        vdst = T["vwin"][:, seq * 2048 + c * 256: seq * 2048 + (c + 1) * 256]
        p.s.emit(lambda s, o=kdst, i=ksrc: s.copy(out=o, in_=i))
        p.s.mark(lambda s, o=vdst, i=vsrc: s.copy(out=o, in_=i))
    p.state["win_update"] = (p.s, p.s.n)
    for c in range(8):
        sl = c % 3
        if c >= 3:
            p.y.wait(p.t, p.state[("bwo_pe", sl)])
        dst = T["wbuf_2"][:, sl * 1024:(sl + 1) * 1024]
        ld = p.dma("y", ("wout", sl), dst, T["s_wout"][c])
        p.t.wait(p.stream(("wout", sl)), ld)
        if c == 0:
            p.t.wait(p.s, ab_ready_s)
            p.t.wait(p.v)

        def t_wo(t, c=c, sl=sl):
            last = None
            for lc in range(4):
                for dhf in range(2):
                    last = t.matmul(BK[lc * 2 + dhf],
                                    ABT[:, c * 512 + lc * 128: c * 512 + (lc + 1) * 128],
                                    T["wbuf_2"][:, sl * 1024 + dhf * 512: sl * 1024 + (dhf + 1) * 512],
                                    start=(c == 0), stop=(c == 7))
            return last

        p.t.mark(t_wo)
        p.state[("bwo_pe", sl)] = p.t.n
    p.v.wait(p.t)
    for lc in range(4):
        xs = T["X"][:, seq * 4096 + lc * D: seq * 4096 + (lc + 1) * D]
        b = T["P8"][:, lc * 1024:(lc + 1) * 1024]
        p.v.mark(lambda v, xs=xs, b=b: v.tensor_add(out=xs, in0=xs, in1=b))
    p.state[("X", seq)] = (p.v, p.v.n)
    p.state["xlf_free"] = [(p.t, p.t.n)]
    p.state["banks_free"] = (p.v, p.v.n)


def store_segment(nc, p, T, seq, n):
    xq, xv = p.state[("X", seq)]
    p.g.wait(xq, xv)
    xs = T["X"][:, seq * 4096:(seq + 1) * 4096].rearrange("p (lc d) -> p lc d", lc=4)
    dst = T["yout"][seq, n * SEG:(n + 1) * SEG, :].rearrange("(lc q) d -> q lc d", q=128)
    st = p.dma("g", ("x", seq), dst, xs)
    p.state[("Xstore", seq)] = st
    if n + 1 < _NSEG_RUN:
        p.g.wait(p.stream(("x", seq)), st)
        src2 = T["xin"][seq, (n + 1) * SEG:(n + 2) * SEG, :].rearrange("(lc q) d -> q lc d", q=128)
        ld = p.dma("g", ("x", seq), xs, src2)
        p.state[("X", seq)] = (p.stream(("x", seq)), ld)


def epilogue(nc, p, T):
    for key, ctr in p.dstream.items():
        p.g.wait(ctr, ctr.n)



def kernel(L0_attn_a_x, L0_attn_a_Wq, L0_attn_a_Wk, L0_attn_a_Wv, L0_attn_a_Wo,
           L0_attn_a_betas, L0_attn_b_w_qkv, L0_attn_b_w_out, L0_ffn_w_gate,
           L0_ffn_w_up, L0_ffn_w_down):
    x = np.ascontiguousarray(L0_attn_a_x)
    nc = build_program()
    in_maps = []
    for c in range(8):
        in_maps.append({
            "x": np.ascontiguousarray(x[2 * c:2 * c + 2]),
            "wq": np.ascontiguousarray(L0_attn_a_Wq),
            "wk": np.ascontiguousarray(L0_attn_a_Wk),
            "wv": np.ascontiguousarray(L0_attn_a_Wv),
            "wo": np.ascontiguousarray(L0_attn_a_Wo),
            "betas": np.ascontiguousarray(L0_attn_a_betas),
            "wqkv": np.ascontiguousarray(L0_attn_b_w_qkv),
            "wout": np.ascontiguousarray(L0_attn_b_w_out),
            "wg": np.ascontiguousarray(L0_ffn_w_gate),
            "wu": np.ascontiguousarray(L0_ffn_w_up),
            "wd": np.ascontiguousarray(L0_ffn_w_down),
        })
    res = run_bass_kernel_spmd(nc, in_maps, core_ids=list(range(8)))
    return np.concatenate([r["y"] for r in res.results], axis=0)


# version 11
# speedup vs baseline: 1.1732x; 1.1732x over previous
import numpy as np
from contextlib import ExitStack
import concourse.bass as bass
import concourse.mybir as mybir
from concourse.bass_utils import run_bass_kernel_spmd

F32 = mybir.dt.float32
BF16 = mybir.dt.bfloat16
I32 = mybir.dt.int32
AF = mybir.ActivationFunctionType
ALU = mybir.AluOpType
AX = mybir.AxisListType

D = 1024
SEG = 512
NSEG = 8
NSEQ = 2
H = 16
DH = 64
HID = 3072
WIN = 256
QSCALE = 1.0 / 8.0
DQ = 1.0 / 127.0
EPS = 1e-5

_NSEG_RUN = NSEG
_SUBLAYERS = (1, 2, 3, 4)
_AA_LIMIT = 99
_DEBUG_SHAPE = None
_DEBUG_EMIT = None


class EngQ:

    def __init__(self, name, sem, self_chain=False):
        self.name = name
        self.sem = sem
        self.n = 0
        self.q = []
        self.self_chain = self_chain

    def emit(self, fn):
        if self.self_chain:
            return self.mark(fn)
        self.q.append(fn)

    def mark(self, fn, inc=1):
        if self.self_chain and self.n > 0:
            sem0, tgt = self.sem, self.n
            self.q.append(lambda e, sem=sem0, val=tgt: e.wait_ge(sem, val))
        self.n += inc
        sem, k = self.sem, inc
        self.q.append(lambda e, fn=fn, sem=sem, k=k: fn(e).then_inc(sem, k))
        return self.n

    def emit_raw(self, fn):
        self.q.append(fn)

    def wait(self, other, val=None):
        if val is None:
            val = other.n
        if val <= 0:
            return
        sem = other.sem
        self.q.append(lambda e, sem=sem, val=val: e.wait_ge(sem, val))


class Prog:
    def __init__(self, nc, sems, mksem):
        self.nc = nc
        self.t = EngQ("tensor", sems["pe"])
        self.v = EngQ("vector", sems["dve"], self_chain=True)
        self.s = EngQ("scalar", sems["act"], self_chain=True)
        self.g = EngQ("gpsimd", sems["pool"], self_chain=True)
        self.y = EngQ("sync", sems["dma_w"])
        self.state = {}
        self._mksem = mksem
        self.dstream = {}

    def stream(self, key):
        if key not in self.dstream:
            self.dstream[key] = EngQ(f"dma_{key}", self._mksem(f"dma_{key}"))
        return self.dstream[key]

    def dma(self, q, stream, out, in_, **kw):
        eq = {"y": self.y, "g": self.g, "s": self.s}[q]
        ctr = self.stream(stream)
        if eq.self_chain and eq.n > 0:
            sem0, tgt = eq.sem, eq.n
            eq.q.append(lambda e, sem=sem0, val=tgt: e.wait_ge(sem, val))
        ctr.n += 16
        sem = ctr.sem
        eq.q.append(lambda e, out=out, in_=in_, sem=sem, kw=kw:
                    e.dma_start(out=out, in_=in_, **kw).then_inc(sem, 16))
        return ctr.n

    def barrier(self, consumers, producers):
        for c in consumers:
            for pr in producers:
                if pr.n > 0:
                    c.wait(pr)


def build_program():
    nc = bass.Bass("TRN2", target_bir_lowering=False)
    T = {}

    T["xin"] = nc.dram_tensor("x", [NSEQ, NSEG * SEG, D], F32, kind="ExternalInput").ap()
    T["wq_d"] = nc.dram_tensor("wq", [D, D], F32, kind="ExternalInput").ap()
    T["wk_d"] = nc.dram_tensor("wk", [D, D], F32, kind="ExternalInput").ap()
    T["wv_d"] = nc.dram_tensor("wv", [D, D], F32, kind="ExternalInput").ap()
    T["wo_d"] = nc.dram_tensor("wo", [D, D], F32, kind="ExternalInput").ap()
    T["betas_d"] = nc.dram_tensor("betas", [1, H, 1, DH], F32, kind="ExternalInput").ap()
    T["wqkv_d"] = nc.dram_tensor("wqkv", [D, 3 * D], F32, kind="ExternalInput").ap()
    T["wout_d"] = nc.dram_tensor("wout", [D, D], F32, kind="ExternalInput").ap()
    T["wg_d"] = nc.dram_tensor("wg", [HID, D], I32, kind="ExternalInput").ap()
    T["wu_d"] = nc.dram_tensor("wu", [HID, D], I32, kind="ExternalInput").ap()
    T["wd_d"] = nc.dram_tensor("wd", [D, HID], I32, kind="ExternalInput").ap()
    T["yout"] = nc.dram_tensor("y", [NSEQ, NSEG * SEG, D], F32, kind="ExternalOutput").ap()
    if _DEBUG_SHAPE is not None:
        T["dbg"] = nc.dram_tensor("dbg", list(_DEBUG_SHAPE), F32, kind="ExternalOutput").ap()

    T["s_wqa"] = nc.dram_tensor("s_wqa", [H, 8, 128, DH], BF16, kind="Internal").ap()
    T["s_wka"] = nc.dram_tensor("s_wka", [H, 8, 128, DH], BF16, kind="Internal").ap()
    T["s_wva"] = nc.dram_tensor("s_wva", [H, 8, 128, DH], BF16, kind="Internal").ap()
    T["s_wo"] = nc.dram_tensor("s_wo", [16, 64, D], BF16, kind="Internal").ap()
    T["s_wqkv"] = nc.dram_tensor("s_wqkv", [24, 8, 128, 128], BF16, kind="Internal").ap()
    T["s_wout"] = nc.dram_tensor("s_wout", [8, 128, D], BF16, kind="Internal").ap()
    T["s_wg"] = nc.dram_tensor("s_wg", [24, 8, 128, 128], BF16, kind="Internal").ap()
    T["s_wu"] = nc.dram_tensor("s_wu", [24, 8, 128, 128], BF16, kind="Internal").ap()
    T["s_wd"] = nc.dram_tensor("s_wd", [24, 8, 128, 128], BF16, kind="Internal").ap()
    T["s_zc"] = nc.dram_tensor("s_zc", [DH, H], F32, kind="Internal").ap()

    with ExitStack() as ctx:
        def sb(name, shape, dt):
            return ctx.enter_context(nc.sbuf_tensor(name, shape, dt)).ap()

        def psv(name, shape, dt=F32):
            return ctx.enter_context(nc.psum_tensor(name, shape, dt)).ap()

        T["X"] = sb("X", [128, NSEQ * 4 * D], F32)
        T["kwin"] = sb("kwin", [128, NSEQ * 8 * WIN], BF16)
        T["vwin"] = sb("vwin", [128, NSEQ * 8 * WIN], BF16)
        T["mem_sb"] = sb("mem_sb", [128, NSEQ * H * DH], F32)
        T["membf"] = sb("membf", [128, DH], BF16)
        T["zc_sb"] = sb("zc_sb", [128, NSEQ * H], F32)
        T["invg"] = sb("invg", [128, H], F32)
        T["grow"] = sb("grow", [1, D], F32)
        T["ident"] = sb("ident", [128, 128], BF16)
        T["ident64"] = sb("ident64", [128, 64], BF16)
        T["ident64a"] = sb("ident64a", [128, 64], BF16)
        T["ident64f"] = sb("ident64f", [128, 64], F32)
        T["onesc"] = sb("onesc", [128, 64], BF16)
        T["onesf"] = sb("onesf", [1, 1], F32)
        T["mask_std"] = sb("mask_std", [128, 384], BF16)
        T["mask_f0"] = sb("mask_f0", [128, 384], BF16)
        T["mask_f1"] = sb("mask_f1", [128, 384], BF16)
        T["c12col"] = sb("c12col", [128, 1], F32)
        T["lncols"] = sb("lncols", [128, 32], F32)
        T["epscol"] = sb("epscol", [128, 1], F32)
        T["cols"] = sb("cols", [128, 32], F32)
        T["rows"] = sb("rows", [128, 1536], F32)
        T["rows_bf"] = sb("rows_bf", [1, 1024], BF16)
        T["TSB"] = sb("TSB", [128, 256], F32)
        T["ESC"] = sb("ESC", [128, 2048], BF16)
        T["BSB"] = sb("BSB", [128, 2 * 256], F32)
        T["SSB"] = sb("SSB", [128, 2 * 512], BF16)

        T["wbuf_a"] = sb("wbuf_a", [128, 2 * 3 * 512], BF16)
        T["wo_buf"] = sb("wo_buf", [128, 2 * D], BF16)
        T["wbuf_1"] = sb("wbuf_1", [128, 3 * 2048], BF16)
        T["wbuf_2"] = sb("wbuf_2", [128, 3 * 1024], BF16)

        AR = 99328
        arena = ctx.enter_context(nc.sbuf_tensor("arena", [128, AR], mybir.dt.uint8)).ap()

        def carve(off, shape, dt):
            nbytes = int(np.prod(shape[1:])) * mybir.dt.size(dt)
            assert off + nbytes <= AR, (off, nbytes, AR)
            return arena[:, off:off + nbytes].bitcast(dt)

        T["xlt"] = carve(0, [128, 2 * D], BF16)
        T["xlf"] = carve(4096, [128, 8 * SEG], BF16)
        T["QA"] = carve(12288, [128, H * SEG], BF16)
        T["KA"] = carve(28672, [128, H * SEG], BF16)
        T["VA"] = carve(45056, [128, H * SEG], BF16)
        T["SQA"] = carve(61440, [128, H * SEG], BF16)
        T["A2"] = carve(77824, [128, 8 * SEG], BF16)
        T["PB"] = carve(86016, [128, 8 * SEG], BF16)
        T["PT"] = carve(94208, [128, 4 * SEG], BF16)
        T["VTST"] = carve(98304, [128, 512], BF16)
        T["xlf2"] = carve(12288, [128, 8 * SEG], BF16)
        T["HA"] = carve(20480, [128, 24 * SEG], BF16)
        T["HB"] = carve(45056, [128, 24 * SEG], BF16)
        T["QKVB"] = carve(12288, [128, 24 * SEG], BF16)
        T["ABT"] = carve(36864, [128, 8 * SEG], BF16)
        T["PB2"] = carve(45056, [128, 8 * 384], BF16)
        T["PT2"] = carve(51200, [128, 4 * 384], BF16)
        T["VT2"] = carve(54272, [128, 2 * 6 * DH], BF16)
        T["cv_i"] = carve(12288, [128, 3 * D], I32)
        T["cv_f"] = carve(24576, [128, 3 * D], F32)
        T["cv_b"] = carve(36864, [128, 3 * D], BF16)
        T["cv_t"] = carve(43008, [128, 1024], BF16)

        P8 = psv("P8", [128, 4096], F32)
        T["P8"] = P8
        T["BK"] = [P8[:, i * 512:(i + 1) * 512] for i in range(8)]

        sems = {}
        for name in ["pe", "dve", "act", "pool", "dma_w"]:
            sems[name] = ctx.enter_context(nc.semaphore(name))
        mksem = lambda nm: ctx.enter_context(nc.semaphore(nm))
        p = Prog(nc, sems, mksem)

        build_phases(nc, p, T)

        with nc.Block() as block:
            @block.tensor
            def _(e):
                for f in p.t.q:
                    f(e)

            @block.vector
            def _(e):
                for f in p.v.q:
                    f(e)

            @block.scalar
            def _(e):
                for f in p.s.q:
                    f(e)

            @block.gpsimd
            def _(e):
                for f in p.g.q:
                    f(e)

            @block.sync
            def _(e):
                for f in p.y.q:
                    f(e)

    return nc



def build_phases(nc, p, T):
    prologue(nc, p, T)
    for n in range(_NSEG_RUN):
        if 1 in _SUBLAYERS:
            for s in range(NSEQ):
                ln_into_xlf(nc, p, T, seq=s, xlf=T["xlf"])
                attn_a(nc, p, T, seq=s, n=n)
        if 2 in _SUBLAYERS:
            ffn(nc, p, T)
        if 3 in _SUBLAYERS:
            for s in range(NSEQ):
                ln_into_xlf(nc, p, T, seq=s, xlf=T["xlf"])
                attn_b(nc, p, T, seq=s, n=n)
        if 4 in _SUBLAYERS:
            ffn(nc, p, T)
        for s in range(NSEQ):
            store_segment(nc, p, T, seq=s, n=n)
    if _DEBUG_EMIT is not None:
        _DEBUG_EMIT(nc, p, T)
    epilogue(nc, p, T)


def prologue(nc, p, T):
    BK = T["BK"]
    ident, ident64, ident64a, onesc = T["ident"], T["ident64"], T["ident64a"], T["onesc"]

    def asel(out, in_, cmp, fill, base, pattern, chm):
        p.g.emit(lambda g, out=out, in_=in_, cmp=cmp, fill=fill, base=base,
                 pattern=pattern, chm=chm: g.affine_select(
                     out=out, in_=in_, compare_op=cmp, fill=fill, base=base,
                     pattern=pattern, channel_multiplier=chm))

    def gmemset(ap_, val):
        p.g.emit(lambda g, ap_=ap_, val=val: g.memset(ap_, val))

    gmemset(ident, 0.0)
    asel(ident, ident, ALU.not_equal, 1.0, 0, [[-1, 128]], 1)
    gmemset(ident64, 0.0)
    asel(ident64, ident64, ALU.not_equal, 1.0, -64, [[-1, 64]], 1)
    gmemset(ident64a, 0.0)
    asel(ident64a, ident64a, ALU.not_equal, 1.0, 0, [[-1, 64]], 1)
    gmemset(T["ident64f"], 0.0)
    asel(T["ident64f"], T["ident64f"], ALU.not_equal, 1.0, -64, [[-1, 64]], 1)
    gmemset(onesc, 1.0)
    for m in (T["mask_std"], T["mask_f0"], T["mask_f1"]):
        gmemset(m, -12.0)
        asel(m, m, ALU.is_ge, -30012.0, -1, [[1, 384]], -1)
        asel(m, m, ALU.is_ge, -30012.0, 256, [[-1, 384]], 1)
    asel(T["mask_f0"], T["mask_f0"], ALU.is_ge, -30012.0, -256, [[1, 384]], 0)
    asel(T["mask_f1"], T["mask_f1"], ALU.is_ge, -30012.0, -128, [[1, 384]], 0)
    gmemset(T["c12col"], -12.0)
    gmemset(T["epscol"], EPS)
    gmemset(T["mem_sb"], 0.0)
    gmemset(T["zc_sb"], 0.0)
    gmemset(T["kwin"], 0.0)
    gmemset(T["vwin"], 0.0)
    gmemset(T["onesf"], 1.0)

    for s in range(NSEQ):
        dst = T["X"][:, s * 4096:(s + 1) * 4096].rearrange("p (lc d) -> p lc d", lc=4)
        src = T["xin"][s, 0:SEG, :].rearrange("(lc q) d -> q lc d", q=128)
        ld = p.dma("g", ("x", s), dst, src)
        p.state[("X", s)] = (p.stream(("x", s)), ld)

    p.dma("s", "misc", T["rows"][0:1, 0:1024], T["betas_d"].rearrange("a h b d -> a (h b d)"))
    p.s.wait(p.stream("misc"))
    p.s.mark(lambda s: s.activation(out=T["grow"], in_=T["rows"][0:1, 0:1024],
                                    func=AF.Sigmoid))
    p.t.wait(p.s)
    p.t.wait(p.g)

    def t_invg(t):
        last = None
        for hp in range(2):
            for h in range(H):
                last = t.matmul(BK[0][hp * 64:hp * 64 + 64, h:h + 1],
                                T["grow"][0:1, h * 64:(h + 1) * 64], T["onesf"],
                                start=True, stop=True,
                                tile_position=(0, hp * 64))
        return last

    p.t.mark(t_invg)
    p.v.wait(p.t)
    p.v.mark(lambda v: v.tensor_scalar(out=T["invg"], in0=BK[0][:, 0:H],
                                       scalar1=-1.0, scalar2=1.0,
                                       op0=ALU.mult, op1=ALU.add))
    bank_free = p.v.n

    cv_f, cv_b, cv_i, cv_t = T["cv_f"], T["cv_b"], T["cv_i"], T["cv_t"]
    halfF = 1536
    halfB = 1536

    conv_marks = []

    def conv_f32(src_rows, n_rows_chunks, width, store_fn, qscale_cols=None):
        for k in range(n_rows_chunks):
            sl = (k % 2) if width <= halfF else 0
            fa = cv_f[:, sl * halfF:sl * halfF + width]
            ba = cv_b[:, sl * halfB:sl * halfB + width]
            p.y.wait(p.stream(("cvS", sl)))
            p.y.wait(p.v)
            p.y.wait(p.s)
            if width > halfF:
                p.y.wait(p.stream(("cvS", 1 - sl)))
            ld = p.dma("y", ("cvL", sl), fa, src_rows(k))
            p.v.wait(p.stream(("cvL", sl)), ld)
            if qscale_cols is None:
                p.v.mark(lambda v, ba=ba, fa=fa: v.tensor_copy(out=ba, in_=fa))
            else:
                c0, c1 = qscale_cols
                if c0 > 0:
                    p.v.emit(lambda v, o=ba[:, 0:c0], i=fa[:, 0:c0]: v.tensor_copy(out=o, in_=i))
                p.v.emit(lambda v, o=ba[:, c0:c1], i=fa[:, c0:c1]: v.tensor_scalar(
                    out=o, in0=i, scalar1=QSCALE, scalar2=None, op0=ALU.mult))
                if c1 < width:
                    p.v.mark(lambda v, o=ba[:, c1:width], i=fa[:, c1:width]: v.tensor_copy(out=o, in_=i))
                else:
                    p.v.mark(lambda v: v.engine_nop())
            store_fn(k, sl, ba, p.v.n)

    for w_d, s_w, qs in ((T["wq_d"], T["s_wqa"], (0, D)), (T["wk_d"], T["s_wka"], None),
                         (T["wv_d"], T["s_wva"], None)):
        def store(k, sl, ba, vmark, s_w=s_w):
            out = s_w[:, k, :, :].transpose([1, 0, 2])
            in_ = ba.rearrange("p (g j) -> p g j", g=16)
            p.s.wait(p.v, vmark)
            p.dma("s", ("cvS", sl), out, in_)

        conv_f32(lambda k, w_d=w_d: w_d[k * 128:(k + 1) * 128, :], 8, D, store,
                 qscale_cols=qs)

    def store_wo(k, sl, ba, vmark):
        p.s.wait(p.v, vmark)
        p.dma("s", ("cvS", sl), T["s_wo"][2 * k], ba[0:64, :])
        p.dma("s", ("cvS", sl), T["s_wo"][2 * k + 1], ba[64:128, :])

    conv_f32(lambda k: T["wo_d"][k * 128:(k + 1) * 128, :], 8, D, store_wo)
    p.state["scr_a"] = [(c, c.n) for c in p.dstream.values()] + [(p.s, p.s.n), (p.g, p.g.n)]

    def store_wqkv(k, sl, ba, vmark):
        out = T["s_wqkv"][:, k, :, :].transpose([1, 0, 2])
        in_ = ba.rearrange("p (c j) -> p c j", c=24)
        p.s.wait(p.v, vmark)
        p.dma("s", ("cvS", sl), out, in_)

    conv_f32(lambda k: T["wqkv_d"][k * 128:(k + 1) * 128, :], 8, 3 * D, store_wqkv,
             qscale_cols=(0, D))

    def store_wout(k, sl, ba, vmark):
        p.s.wait(p.v, vmark)
        p.dma("s", ("cvS", sl), T["s_wout"][k], ba)

    conv_f32(lambda k: T["wout_d"][k * 128:(k + 1) * 128, :], 8, D, store_wout)
    p.state["scr_b"] = [(c, c.n) for c in p.dstream.values()] + [(p.s, p.s.n), (p.g, p.g.n)]

    p.t.wait(p.v, bank_free)

    def conv_int(src_rows, n_chunks, width, store_fn):
        for k in range(n_chunks):
            sl = (k % 2) if width <= halfF else 0
            ia = cv_i[:, sl * halfF:sl * halfF + width]
            fa = cv_f[:, sl * halfF:sl * halfF + width]
            ba = cv_b[:, sl * halfB:sl * halfB + width]
            p.y.wait(p.t)
            p.y.wait(p.v)
            p.y.wait(p.s)
            if width > halfF:
                p.y.wait(p.stream(("cvS", 1 - sl)))
            p.y.wait(p.stream(("cvS", sl)))
            ld = p.dma("y", ("cvL", sl), ia, src_rows(k))
            p.v.wait(p.stream(("cvL", sl)), ld)
            p.v.mark(lambda v, fa=fa, ia=ia: v.tensor_copy(out=fa, in_=ia))
            p.s.wait(p.v)
            p.s.mark(lambda s, ba=ba, fa=fa: s.activation(out=ba, in_=fa,
                                                          func=AF.Copy, scale=DQ))
            store_fn(k, ba, p.s.n)

    for w_d, s_w in ((T["wg_d"], T["s_wg"]), (T["wu_d"], T["s_wu"])):
        def store_g(r, ba, smark, s_w=s_w):
            p.t.wait(p.s, smark)
            p.t.wait(p.stream("cvt"))

            def t_fn(t, ba=ba):
                last = None
                for k in range(8):
                    last = t.transpose(BK[0].bitcast(BF16)[:, k * 128:(k + 1) * 128],
                                       ba[:, k * 128:(k + 1) * 128], T["ident"])
                return last

            p.t.mark(t_fn)
            p.v.wait(p.t)
            p.v.mark(lambda v: v.tensor_copy(out=cv_t, in_=BK[0].bitcast(BF16)[:, 0:1024]))
            p.g.wait(p.v)
            out = s_w[r].transpose([1, 0, 2])
            p.dma("g", "cvt", out, cv_t.rearrange("p (k j) -> p k j", k=8))

        conv_int(lambda r, w_d=w_d: w_d[r * 128:(r + 1) * 128, :], 24, D, store_g)

    def store_wd(r, ba, smark):
        p.t.wait(p.s, smark)
        for half in range(3):
            p.t.wait(p.stream("cvt"))

            def t_fn(t, ba=ba, half=half):
                last = None
                for kk in range(8):
                    c = half * 8 + kk
                    last = t.transpose(BK[0].bitcast(BF16)[:, kk * 128:(kk + 1) * 128],
                                       ba[:, c * 128:(c + 1) * 128], T["ident"])
                return last

            p.t.mark(t_fn)
            p.v.wait(p.t)
            p.v.mark(lambda v: v.tensor_copy(out=cv_t, in_=BK[0].bitcast(BF16)[:, 0:1024]))
            p.g.wait(p.v)
            for kk in range(8):
                c = half * 8 + kk
                p.dma("g", "cvt", T["s_wd"][c, r], cv_t.rearrange("p (k j) -> p k j", k=8)[:, kk, :])

    conv_int(lambda r: T["wd_d"][r * 128:(r + 1) * 128, :], 8, HID, store_wd)

    p.state["scratch"] = ([(c, c.n) for c in p.dstream.values()] +
                          [(p.g, p.g.n), (p.v, p.v.n), (p.s, p.s.n), (p.t, p.t.n)])


def wait_scratch(p, eng, key="scratch"):
    for q, val in p.state[key]:
        eng.wait(q, val)


def ln_into_xlf(nc, p, T, seq, xlf):
    X, xlt, lnc, BK = T["X"], T["xlt"], T["lncols"], T["BK"]
    xq = p.state.get(("X", seq), (None, 0))
    if xq[0] is not None:
        p.v.wait(xq[0], xq[1])
        p.s.wait(xq[0], xq[1])
    p.s.wait(p.t)
    X4 = X[:, seq * 4096:(seq + 1) * 4096]
    p.v.mark(lambda v: v.reduce_sum(out=lnc[:, 0:4],
                                    in_=X4.rearrange("p (lc d) -> p lc d", lc=4),
                                    axis=AX.X))
    for lc in range(4):
        Xt = X4[:, lc * D:(lc + 1) * D]
        xs = xlt[:, (lc % 2) * D:(lc % 2 + 1) * D]
        p.s.mark(lambda s, Xt=Xt, xs=xs, c=lnc[:, 4 + lc:5 + lc]: s.activation(
            out=xs, in_=Xt, func=AF.Square, accum_out=c))
    p.v.wait(p.s)
    p.v.emit(lambda v: v.tensor_scalar(out=lnc[:, 8:12], in0=lnc[:, 0:4],
                                       scalar1=1.0 / D, scalar2=None, op0=ALU.mult))
    p.v.emit(lambda v: v.tensor_mul(out=lnc[:, 12:16], in0=lnc[:, 8:12], in1=lnc[:, 8:12]))
    p.v.mark(lambda v: v.scalar_tensor_tensor(out=lnc[:, 16:20], in0=lnc[:, 4:8],
                                              scalar=1.0 / D, in1=lnc[:, 12:16],
                                              op0=ALU.mult, op1=ALU.subtract))
    p.s.wait(p.v)
    p.s.mark(lambda s: s.activation(out=lnc[:, 20:24], in_=lnc[:, 16:20],
                                    func=AF.Sqrt, bias=T["epscol"]))
    p.v.wait(p.s)
    p.v.emit(lambda v: v.reciprocal(out=lnc[:, 24:28], in_=lnc[:, 20:24]))
    p.v.mark(lambda v: v.scalar_tensor_tensor(out=lnc[:, 28:32], in0=lnc[:, 8:12],
                                              scalar=-1.0, in1=lnc[:, 24:28],
                                              op0=ALU.mult, op1=ALU.mult))
    for lc in range(4):
        Xt = X4[:, lc * D:(lc + 1) * D]
        xs = xlt[:, (lc % 2) * D:(lc % 2 + 1) * D]
        if lc >= 2:
            p.v.wait(p.t)
        p.v.mark(lambda v, Xt=Xt, xs=xs, c6=lnc[:, 24 + lc:25 + lc],
                 c7=lnc[:, 28 + lc:29 + lc]: v.tensor_scalar(
                     out=xs, in0=Xt, scalar1=c6, scalar2=c7,
                     op0=ALU.mult, op1=ALU.add))
        bank = BK[6 + (lc % 2)]
        p.t.wait(p.v)
        p.t.wait(p.v, p.state.get(("lnbank", 6 + lc % 2), 0))

        def t_fn(t, xs=xs, bank=bank):
            last = None
            for cc in range(8):
                last = t.transpose(bank.bitcast(BF16)[:, cc * 128:(cc + 1) * 128],
                                   xs[:, cc * 128:(cc + 1) * 128], T["ident"])
            return last

        p.t.mark(t_fn)
        p.v.wait(p.t)
        out = xlf.rearrange("p (cc l) -> p cc l", cc=8)[:, :, lc * 128:(lc + 1) * 128]
        p.v.mark(lambda v, out=out, bank=bank: v.tensor_copy(
            out=out, in_=bank.bitcast(BF16)[:, 0:1024].rearrange("p (cc l) -> p cc l", cc=8)))
        p.state[("lnbank", 6 + lc % 2)] = p.v.n
    p.state["xlf"] = (p.v, p.v.n)


def attn_a(nc, p, T, seq, n):
    BK = T["BK"]
    QA, KA, VA, SQA, A2 = T["QA"], T["KA"], T["VA"], T["SQA"], T["A2"]
    PB, PT = T["PB"], T["PT"]
    rows, cols = T["rows"], T["cols"]
    wbuf, wo_buf = T["wbuf_a"], T["wo_buf"]
    wait_scratch(p, p.y)
    p.y.wait(p.t)
    p.t.wait(p.g)
    p.v.wait(p.t)
    p.s.wait(p.t)
    xq, xv = p.state["xlf"]
    p.t.wait(xq, xv)

    ev_marks = {}
    for g in range(16):
        sl = g % 2
        b0, b1 = BK[sl * 2], BK[sl * 2 + 1]
        if g >= 2:
            pe_done = p.state[("aproj_pe", sl)]
            p.y.wait(p.t, pe_done)
        for idx, s_w in enumerate((T["s_wqa"], T["s_wka"], T["s_wva"])):
            dst = wbuf[:, sl * 1536 + idx * 512: sl * 1536 + idx * 512 + 512]
            dst = dst.rearrange("p (k j) -> p k j", k=8)
            ld = p.dma("y", ("wA", sl), dst, s_w[g].transpose([1, 0, 2]))
        p.t.wait(p.stream(("wA", sl)), ld)
        if g >= 2:
            for q, val in ev_marks[sl]:
                p.t.wait(q, val)

        def t_fn(t, sl=sl, b0=b0, b1=b1):
            last = None
            for k in range(8):
                last = t.matmul(
                    b0[64:128, :],
                    T["wbuf_a"][:, sl * 1536 + k * 64: sl * 1536 + (k + 1) * 64],
                    T["xlf"][:, k * 512:(k + 1) * 512],
                    start=(k == 0), stop=(k == 7), tile_position=(0, 64))
                last = t.matmul(
                    b0[0:64, :],
                    T["wbuf_a"][:, sl * 1536 + 1024 + k * 64: sl * 1536 + 1024 + (k + 1) * 64],
                    T["xlf"][:, k * 512:(k + 1) * 512],
                    start=(k == 0), stop=(k == 7), tile_position=(0, 0))
            for k in range(8):
                last = t.matmul(
                    b1[64:128, :],
                    T["wbuf_a"][:, sl * 1536 + 512 + k * 64: sl * 1536 + 512 + (k + 1) * 64],
                    T["xlf"][:, k * 512:(k + 1) * 512],
                    start=(k == 0), stop=(k == 7), tile_position=(0, 64))
            return last

        p.t.mark(t_fn)
        p.state[("aproj_pe", sl)] = p.t.n
        sp = (g % 2) * 256 + (g // 2) * 32
        qdst = QA[64:128, :].rearrange("p (h s) -> p h s", h=H)[:, :, sp:sp + 32]
        kdst = KA[64:128, :].rearrange("p (h s) -> p h s", h=H)[:, :, sp:sp + 32]
        vdst = VA[0:64, :].rearrange("p (h s) -> p h s", h=H)[:, :, sp:sp + 32]
        qsrc = b0[64:128, :].rearrange("p (h i) -> p h i", h=H)
        ksrc = b1[64:128, :].rearrange("p (h i) -> p h i", h=H)
        vsrc = b0[0:64, :].rearrange("p (h i) -> p h i", h=H)
        p.v.wait(p.t)
        p.s.wait(p.t)
        p.v.mark(lambda v, o=qdst, i=qsrc: v.tensor_copy(out=o, in_=i))
        p.s.mark(lambda s, o=kdst, i=ksrc: s.copy(out=o, in_=i))
        p.v.mark(lambda v, o=vdst, i=vsrc: v.tensor_copy(out=o, in_=i))
        ev_marks[sl] = [(p.v, p.v.n), (p.s, p.s.n)]
    p.t.wait(p.v)
    p.t.wait(p.s)
    proj_done_v = p.v.n
    proj_done_s = p.s.n

    RF = VA[64:128, :]
    EH = T["ESC"][64:128, :]
    sh = seq * H
    zpart = T["rows"][64:128, 0:16]
    rsum = T["rows"][64:128, 16:32]
    zinc = T["rows"][64:128, 32:48]
    gz = T["rows"][0:1, 512:1536]

    def sqsk_quarter(q):
        blk = slice(q * 2048, (q + 1) * 2048)
        rfq = RF[:, blk]
        ehq = EH
        p.s.wait(p.v)
        p.s.mark(lambda s, rfq=rfq: s.activation(out=rfq, in_=QA[64:128, blk], func=AF.Relu))
        p.v.wait(p.s)
        p.v.mark(lambda v: v.tensor_sub(out=SQA[64:128, blk], in0=QA[64:128, blk], in1=rfq))
        p.s.wait(p.v)
        p.s.mark(lambda s: s.activation(out=ehq, in_=SQA[64:128, blk], func=AF.Exp,
                                        scale=1.0 / QSCALE))
        p.v.wait(p.s)
        p.v.mark(lambda v: v.scalar_tensor_tensor(
            out=SQA[64:128, blk], in0=rfq, scalar=1.0 / QSCALE, in1=ehq,
            op0=ALU.mult, op1=ALU.add))
        p.state[("sqq", q)] = p.v.n
        p.s.wait(p.v)
        p.s.mark(lambda s, rfq=rfq: s.activation(out=rfq, in_=KA[64:128, blk], func=AF.Relu))
        p.v.wait(p.s)
        p.v.emit(lambda v, o=rsum[:, q * 4:(q + 1) * 4], rfq=rfq: v.reduce_sum(
            out=o, in_=rfq.rearrange("p (h l) -> p h l", h=4), axis=AX.X))
        p.v.mark(lambda v: v.tensor_sub(out=ehq, in0=KA[64:128, blk], in1=rfq))
        p.s.wait(p.v)
        p.s.mark(lambda s, rfq=rfq: s.activation(out=rfq, in_=ehq, func=AF.Exp))
        p.v.wait(p.s)
        p.v.emit(lambda v, o=zpart[:, q * 4:(q + 1) * 4], rfq=rfq: v.reduce_sum(
            out=o, in_=rfq.rearrange("p (h l) -> p h l", h=4), axis=AX.X))
        p.v.emit(lambda v, o=zinc[:, q * 4:(q + 1) * 4]: v.scalar_tensor_tensor(
            out=o, in0=rsum[:, q * 4:(q + 1) * 4], scalar=1.0,
            in1=zpart[:, q * 4:(q + 1) * 4], op0=ALU.mult, op1=ALU.add))
        zcs = T["zc_sb"][64:128, sh + q * 4:sh + (q + 1) * 4]
        p.v.mark(lambda v, zcs=zcs: v.tensor_add(out=zcs, in0=zcs,
                                                 in1=zinc[:, q * 4:(q + 1) * 4]))
        zq_ready = p.v.n
        p.y.wait(p.v, zq_ready)
        st1 = p.dma("y", "zc", T["s_zc"][:, q * 4:(q + 1) * 4],
                    T["zc_sb"][64:128, sh + q * 4:sh + (q + 1) * 4])
        p.y.wait(p.stream("zc"), st1)
        gq = slice(q * 256, (q + 1) * 256)
        st2 = p.dma("y", "zc", rows[0:1, 512 + q * 256:512 + (q + 1) * 256],
                    T["s_zc"][:, q * 4:(q + 1) * 4].transpose([1, 0]),
                    allow_slow_non_contiguous=True)
        p.v.wait(p.stream("zc"), st2)
        p.v.emit(lambda v, gq=gq: v.reciprocal(out=gz[:, gq], in_=gz[:, gq]))
        p.v.mark(lambda v, gq=gq: v.tensor_mul(out=gz[:, gq], in0=T["grow"][:, gq],
                                               in1=gz[:, gq]))
        p.state[("gzq", q)] = p.v.n

    AV_BANK, NUM_BANK = 5, 6
    PTB = T["P8"][:, 2 * 512:4 * 512].bitcast(BF16)
    TRB = T["P8"][:, 4 * 512:5 * 512].bitcast(BF16)
    B7 = BK[7]
    VT = T["VTST"][:, 0:256]
    ST = T["VTST"][:, 256:512]
    rows = T["rows"]
    cols = T["cols"]

    def small_pe_1(h):
        p.t.wait(p.v, p.state[("sqq", h // 4)])
        p.t.wait(p.v, p.state.get(("a7",), 0))
        p.t.mark(lambda t, i=SQA[64:128, h * 512:(h + 1) * 512]: t.matmul(
            B7[0:1, 0:512], T["onesc"][64:128, 0:1], i, start=True, stop=True,
            tile_position=(64, 0)))
        p.v.wait(p.t)
        p.v.mark(lambda v: v.reciprocal(out=rows[0:1, 0:512], in_=B7[0:1, 0:512]))
        p.state[("a7",)] = p.v.n

    def small_pe_2(h):
        p.t.wait(p.v, p.state[("a7",)])
        p.t.wait(p.v, p.state[("gzq", h // 4)])

        def t_b(t, h=h):
            last = None
            for pp in range(2):
                last = t.matmul(B7[pp * 64:pp * 64 + 64, 0:256],
                                rows[0:1, 512 + h * 64:512 + (h + 1) * 64],
                                rows[0:1, pp * 256:pp * 256 + 256],
                                start=True, stop=True, tile_position=(0, pp * 64))
            return last

        p.t.mark(t_b)
        p.v.wait(p.t)
        p.v.mark(lambda v, o=T["BSB"][:, (h % 2) * 256:(h % 2) * 256 + 256]:
                 v.tensor_copy(out=o, in_=B7[:, 0:256]))
        p.state[("a7",)] = p.v.n

    def scores_part(h):
        hof = h * 512
        cb = (h % 2) * 8
        for sc in range(4):
            sb_ = sc % 2
            p.t.wait(p.s, p.state.get(("aS_free", sb_), 0))
            p.t.mark(lambda t, b=BK[sb_], lh=QA[64:128, hof + sc * 128:hof + sc * 128 + 128],
                     rh=KA[64:128, hof:hof + 512]: t.matmul(b, lh, rh, start=True, stop=True))
            p.s.wait(p.t)
            p.s.wait(p.t, p.state.get(("aPBt", h % 2), 0))
            pb = PB[:, ((h % 2) * 4 + sc) * 512:((h % 2) * 4 + sc + 1) * 512]
            p.s.mark(lambda s, o=pb, b=BK[sb_], c2=cols[:, cb + sc:cb + sc + 1]:
                     s.activation(out=o, in_=b, func=AF.Exp, bias=T["c12col"], accum_out=c2))
            p.state[("aS_free", sb_)] = p.s.n
        p.v.wait(p.s)
        p.v.mark(lambda v: v.reciprocal(out=cols[:, cb + 4:cb + 8], in_=cols[:, cb:cb + 4]))
        for sc in range(4):
            pb = PB[:, ((h % 2) * 4 + sc) * 512:((h % 2) * 4 + sc + 1) * 512]
            p.v.mark(lambda v, o=pb, c3=cols[:, cb + 4 + sc:cb + 5 + sc]: v.tensor_scalar(
                out=o, in0=o, scalar1=c3, scalar2=None, op0=ALU.mult))
        p.state[("norms", h)] = p.v.n

    def tail_part(h):
        hof = h * 512
        p.t.wait(p.s, p.state.get(("aTR_free",), 0))

        def t_vt4(t, hof=hof):
            last = None
            for j in range(4):
                last = t.transpose(TRB[:, j * 64:(j + 1) * 64],
                                   VA[0:64, hof + j * 128:hof + (j + 1) * 128],
                                   T["ident64a"][0:64, :])
            return last

        def t_st4(t, hof=hof):
            last = None
            for j in range(4):
                last = t.transpose(TRB[:, 256 + j * 64: 256 + (j + 1) * 64],
                                   SQA[64:128, hof + j * 128:hof + (j + 1) * 128],
                                   T["ident64"][64:128, :])
            return last

        m1 = p.t.mark(t_vt4)
        p.t.wait(p.t, m1)
        p.t.mark(t_st4)
        p.s.wait(p.t)
        p.s.mark(lambda s: s.copy(out=T["VTST"], in_=TRB[:, 0:512]))
        vtst_ready = p.s.n
        p.state[("aTR_free",)] = p.s.n
        p.t.wait(p.s, vtst_ready)
        p.t.wait(p.v, p.state.get(("a7",), 0))

        def t_mem(t):
            last = None
            for j in range(4):
                last = t.matmul(B7[64:128, 256:320], ST[:, j * 64:(j + 1) * 64],
                                VT[:, j * 64:(j + 1) * 64],
                                start=(j == 0), stop=(j == 3), tile_position=(0, 64))
            return last

        p.t.mark(t_mem)
        p.v.wait(p.t)
        mm = T["mem_sb"][64:128, (sh + h) * 64:(sh + h + 1) * 64]
        p.v.mark(lambda v, mm=mm: v.tensor_add(out=mm, in0=mm, in1=B7[64:128, 256:320]))
        p.v.mark(lambda v, mm=mm: v.tensor_copy(out=T["membf"][64:128, :], in_=mm))
        mem_ready = p.v.n
        p.state[("a7",)] = p.v.n
        p.t.wait(p.v, p.state[("norms", h)])
        p.t.wait(p.v, p.state.get(("aAV_free",), 0))
        evac_marks = {}
        for sc in range(5):
            if sc < 4:
                tb = PTB[:, (sc % 2) * 1024:(sc % 2) * 1024 + 512]
                p.t.wait(p.s, p.state.get(("aPT_free", sc % 2), 0))

                def t_pt4(t, h=h, sc=sc, tb=tb):
                    last = None
                    for j in range(4):
                        last = t.transpose(tb[:, j * 128:(j + 1) * 128],
                                           PB[:, ((h % 2) * 4 + sc) * 512 + j * 128:
                                              ((h % 2) * 4 + sc) * 512 + (j + 1) * 128],
                                           T["ident"])
                    return last

                p.t.mark(t_pt4)
                p.s.wait(p.t)
                out = PT.rearrange("p (j s2) -> p j s2", j=4)[:, :, sc * 128:(sc + 1) * 128]
                p.s.mark(lambda s, out=out, tb=tb: s.copy(
                    out=out, in_=tb.rearrange("p (j c) -> p j c", j=4)))
                p.state[("aPT_free", sc % 2)] = p.s.n
                evac_marks[sc] = p.s.n
            if sc >= 1:
                sp_ = sc - 1
                pp = sp_ // 2
                pq = 1 - pp
                p.t.wait(p.s, evac_marks[sp_])

                def t_av4(t, sp_=sp_, pp=pp):
                    last = None
                    for j in range(4):
                        last = t.matmul(BK[AV_BANK][pp * 64:pp * 64 + 64, sp_ * 128:(sp_ + 1) * 128],
                                        VT[:, j * 64:(j + 1) * 64],
                                        PT[:, j * 512 + sp_ * 128: j * 512 + (sp_ + 1) * 128],
                                        start=(j == 0), stop=(j == 3), tile_position=(0, pp * 64))
                    return last

                p.t.mark(t_av4)
        p.state[("aPBt", h % 2)] = p.t.n
        p.t.wait(p.v, mem_ready)
        p.t.wait(p.v, p.state.get(("aNUM_free",), 0))

        def t_num(t, hof=hof):
            last = None
            for sc in (0, 2, 1, 3):
                pp = sc // 2
                last = t.matmul(BK[NUM_BANK][pp * 64:pp * 64 + 64, sc * 128:(sc + 1) * 128],
                                T["membf"][64:128, :],
                                SQA[64:128, hof + sc * 128:hof + (sc + 1) * 128],
                                start=True, stop=True, tile_position=(64, pp * 64))
            return last

        p.t.mark(t_num)
        p.v.wait(p.t)
        for pp in range(2):
            tsb = T["TSB"][pp * 64:pp * 64 + 64, :]
            p.v.emit(lambda v, o=tsb, nu=BK[NUM_BANK][pp * 64:pp * 64 + 64, pp * 256:pp * 256 + 256],
                     bs=T["BSB"][pp * 64:pp * 64 + 64, (h % 2) * 256:(h % 2) * 256 + 256]:
                     v.scalar_tensor_tensor(out=o, in0=nu, scalar=1.0, in1=bs,
                                            op0=ALU.mult, op1=ALU.mult))
            a2o = A2[pp * 64:pp * 64 + 64, :].rearrange("p (cc l) -> p cc l", cc=8)[:, :, h * 32:(h + 1) * 32]
            p.v.mark(lambda v, o=a2o, av=BK[AV_BANK][pp * 64:pp * 64 + 64, pp * 256:pp * 256 + 256],
                     ig=T["invg"][pp * 64:pp * 64 + 64, h:h + 1], ts=tsb:
                     v.scalar_tensor_tensor(out=o, in0=av, scalar=ig, in1=ts,
                                            op0=ALU.mult, op1=ALU.add))
        p.state[("aAV_free",)] = p.v.n
        p.state[("aNUM_free",)] = p.v.n

    for h in range(H if _AA_LIMIT >= 1 else 0):
        if h % 4 == 0:
            sqsk_quarter(h // 4)
        small_pe_1(h)
        scores_part(h)
        if h >= 1:
            tail_part(h - 1)
        small_pe_2(h)
    if _AA_LIMIT >= 1 and H > 0:
        tail_part(H - 1)
    a2_ready = p.v.n

    for g in range(16 if _AA_LIMIT >= 7 else 0):
        sl = g % 2
        pp = g % 2
        cc = g // 2
        if g >= 2:
            p.y.wait(p.t, p.state[("awo_pe", sl)])
        dst = wo_buf[pp * 64:pp * 64 + 64, sl * D:(sl + 1) * D]
        ld = p.dma("y", ("wo", sl), dst, T["s_wo"][g])
        p.t.wait(p.stream(("wo", sl)), ld)
        if g == 0:
            p.t.wait(p.v, a2_ready)

        def t_wo(t, g=g, sl=sl, pp=pp, cc=cc):
            last = None
            for lc in range(4):
                for dhf in range(2):
                    last = t.matmul(BK[lc * 2 + dhf],
                                    A2[pp * 64:pp * 64 + 64, cc * 512 + lc * 128: cc * 512 + (lc + 1) * 128],
                                    wo_buf[pp * 64:pp * 64 + 64, sl * D + dhf * 512: sl * D + (dhf + 1) * 512],
                                    start=(g == 0), stop=(g == 15), tile_position=(pp * 64, 0))
            return last

        p.t.mark(t_wo)
        p.state[("awo_pe", sl)] = p.t.n
    if _AA_LIMIT >= 7:
        p.v.wait(p.t)
        for lc in range(4):
            xs = T["X"][:, seq * 4096 + lc * D: seq * 4096 + (lc + 1) * D]
            b = T["P8"][:, lc * 1024:(lc + 1) * 1024]
            p.v.mark(lambda v, xs=xs, b=b: v.tensor_add(out=xs, in0=xs, in1=b))
        p.state[("X", seq)] = (p.v, p.v.n)
    p.state["xlf_free"] = [(p.t, p.t.n)]
    p.state["banks_free"] = (p.v, p.v.n)


def ffn(nc, p, T):
    BK = T["BK"]
    wait_scratch(p, p.y)
    p.y.wait(p.t)
    p.t.wait(p.g)
    p.v.wait(p.g)
    ln_into_xlf(nc, p, T, seq=0, xlf=T["xlf"])
    xlf_a_mark = p.state["xlf"]
    ln_into_xlf(nc, p, T, seq=1, xlf=T["xlf2"])
    xlf_b_mark = p.state["xlf"]
    p.t.wait(*xlf_a_mark)
    p.t.wait(*xlf_b_mark)
    q, val = p.state.get("banks_free", (None, 0))
    if q is not None:
        p.t.wait(q, val)
    xls = (T["xlf"], T["xlf2"])
    hts = (T["HA"], T["HB"])
    for c in range(24):
        sl = c % 3
        if c >= 3:
            p.y.wait(p.t, p.state[("f1_pe", sl)])
        for idx, s_w in enumerate((T["s_wg"], T["s_wu"])):
            dst = T["wbuf_1"][:, sl * 2048 + idx * 1024: sl * 2048 + (idx + 1) * 1024]
            ld = p.dma("y", ("f1", sl), dst.rearrange("p (k j) -> p k j", k=8),
                       s_w[c].transpose([1, 0, 2]))
        p.t.wait(p.stream(("f1", sl)), ld)
        for sq in range(NSEQ):
            bG = BK[(sl * 2 + sq) % 4]
            bU = BK[4 + (sl * 2 + sq) % 4]
            key = ("f1_banks", (sl * 2 + sq) % 4)
            qv = p.state.get(key, 0)
            p.t.wait(p.v, qv)

            def t_gu(t, sl=sl, sq=sq, bG=bG, bU=bU):
                last = None
                for k in range(8):
                    last = t.matmul(bG, T["wbuf_1"][:, sl * 2048 + k * 128:(sl * 2048 + (k + 1) * 128)],
                                    xls[sq][:, k * 512:(k + 1) * 512],
                                    start=(k == 0), stop=(k == 7))
                for k in range(8):
                    last = t.matmul(bU, T["wbuf_1"][:, sl * 2048 + 1024 + k * 128: sl * 2048 + 1024 + (k + 1) * 128],
                                    xls[sq][:, k * 512:(k + 1) * 512],
                                    start=(k == 0), stop=(k == 7))
                return last

            p.t.mark(t_gu)
            p.state[("f1_pe", sl)] = p.t.n
            ssl = (c * 2 + sq) % 2
            ssb = T["SSB"][:, ssl * 512:(ssl + 1) * 512]
            p.s.wait(p.t)
            p.s.wait(p.v, p.state.get(("ssb_free", ssl), 0))
            p.s.mark(lambda s, bG=bG, ssb=ssb: s.activation(out=ssb, in_=bG, func=AF.Silu))
            p.v.wait(p.s)
            hdst = hts[sq][:, c * 512:(c + 1) * 512]
            p.v.mark(lambda v, o=hdst, bU=bU, ssb=ssb: v.tensor_mul(out=o, in0=ssb, in1=bU))
            p.state[key] = p.v.n
            p.state[("ssb_free", ssl)] = p.v.n
    h_ready = p.v.n
    for sq in range(NSEQ):
        for c in range(24):
            sl = c % 3
            if c >= 3 or sq > 0:
                key = ("f2_pe", sl)
                if key in p.state:
                    p.y.wait(p.t, p.state[key])
            dst = T["wbuf_2"][:, sl * 1024:(sl + 1) * 1024]
            ld = p.dma("y", ("f2", sl), dst.rearrange("p (k j) -> p k j", k=8),
                       T["s_wd"][c].transpose([1, 0, 2]))
            p.t.wait(p.stream(("f2", sl)), ld)
            if c == 0:
                p.t.wait(p.v, h_ready if sq == 0 else p.state[("f2_res", 0)])

            def t_dn(t, sq=sq, c=c, sl=sl):
                last = None
                for lc in range(4):
                    for dhf in range(2):
                        last = t.matmul(BK[lc * 2 + dhf],
                                        hts[sq][:, c * 512 + lc * 128: c * 512 + (lc + 1) * 128],
                                        T["wbuf_2"][:, sl * 1024 + dhf * 512: sl * 1024 + (dhf + 1) * 512],
                                        start=(c == 0), stop=(c == 23))
                return last

            p.t.mark(t_dn)
            p.state[("f2_pe", sl)] = p.t.n
        p.v.wait(p.t)
        for lc in range(4):
            xs = T["X"][:, sq * 4096 + lc * D: sq * 4096 + (lc + 1) * D]
            b = T["P8"][:, lc * 1024:(lc + 1) * 1024]
            p.v.mark(lambda v, xs=xs, b=b: v.tensor_add(out=xs, in0=xs, in1=b))
        p.state[("f2_res", sq)] = p.v.n
        p.state[("X", sq)] = (p.v, p.v.n)
    p.state["xlf_free"] = [(p.t, p.t.n)]
    p.state["banks_free"] = (p.v, p.v.n)


def attn_b(nc, p, T, seq, n):
    BK = T["BK"]
    QKVB, ABT, PB2, PT2, VT2 = T["QKVB"], T["ABT"], T["PB2"], T["PT2"], T["VT2"]
    cols = T["cols"]
    wait_scratch(p, p.y)
    p.y.wait(p.t)
    p.t.wait(p.s)
    p.v.wait(p.s)
    xq, xv = p.state["xlf"]
    p.t.wait(xq, xv)
    q, val = p.state.get("banks_free", (None, 0))
    if q is not None:
        p.t.wait(q, val)
    for c in range(24):
        sl = c % 3
        if c >= 3:
            p.y.wait(p.t, p.state[("bproj_pe", sl)])
        dst = T["wbuf_1"][:, sl * 2048: sl * 2048 + 1024]
        ld = p.dma("y", ("qkv", sl), dst.rearrange("p (k j) -> p k j", k=8),
                   T["s_wqkv"][c].transpose([1, 0, 2]))
        p.t.wait(p.stream(("qkv", sl)), ld)
        bank = BK[c % 2]
        qv = p.state.get(("bproj_banks", c % 2), 0)
        p.t.wait(p.v, qv)
        p.t.wait(p.s, p.state.get(("bproj_banks_s", c % 2), 0))

        def t_p(t, sl=sl, bank=bank):
            last = None
            for k in range(8):
                last = t.matmul(bank, T["wbuf_1"][:, sl * 2048 + k * 128: sl * 2048 + (k + 1) * 128],
                                T["xlf"][:, k * 512:(k + 1) * 512],
                                start=(k == 0), stop=(k == 7))
            return last

        p.t.mark(t_p)
        p.state[("bproj_pe", sl)] = p.t.n
        dstq = QKVB[:, c * 512:(c + 1) * 512]
        if c % 2 == 0:
            p.v.wait(p.t)
            p.v.mark(lambda v, o=dstq, b=bank: v.tensor_copy(out=o, in_=b))
            p.state[("bproj_banks", c % 2)] = p.v.n
        else:
            p.s.wait(p.t)
            p.s.mark(lambda s, o=dstq, b=bank: s.copy(out=o, in_=b))
            p.state[("bproj_banks_s", c % 2)] = p.s.n
    p.t.wait(p.v)
    p.t.wait(p.s)
    qkv_ready_v, qkv_ready_s = p.v.n, p.s.n

    S_BANKS = (0, 1, 6)
    AV_BANK = 5
    PTB2 = T["P8"][:, 2 * 512:4 * 512].bitcast(BF16)
    TRB2 = T["P8"][:, 4 * 512:5 * 512].bitcast(BF16)

    def scores_b(h):
        b = (h % 2) * 64
        cq, ck, cv = h // 2, 8 + h // 2, 16 + h // 2
        idt = T["ident64a"][0:64, :] if b == 0 else T["ident64"][64:128, :]
        cb = 16 + (h % 2) * 8
        p.t.wait(p.s, p.state.get(("bTR_free",), 0))

        def t_vt(t, b=b, cv=cv, idt=idt):
            last = None
            for j in range(2):
                src = T["vwin"][b:b + 64, seq * 2048 + (cv - 16) * 256 + j * 128:
                                seq * 2048 + (cv - 16) * 256 + (j + 1) * 128]
                last = t.transpose(TRB2[:, j * 64:(j + 1) * 64], src, idt)
            for j in range(4):
                src = QKVB[b:b + 64, cv * 512 + j * 128: cv * 512 + (j + 1) * 128]
                last = t.transpose(TRB2[:, (2 + j) * 64:(3 + j) * 64], src, idt)
            return last

        p.t.mark(t_vt)
        p.s.wait(p.t)
        p.s.mark(lambda s, o=VT2[:, (h % 2) * 384:(h % 2) * 384 + 384]:
                 s.copy(out=o, in_=TRB2[:, 0:384]))
        p.state[("vt", h)] = p.s.n
        p.state[("bTR_free",)] = p.s.n
        for qc in range(4):
            sb_ = S_BANKS[(h * 4 + qc) % 3]
            lh = QKVB[b:b + 64, cq * 512 + qc * 128: cq * 512 + (qc + 1) * 128]
            pieces = []
            wbase = seq * 2048 + (ck - 8) * 256
            if qc == 0:
                pieces.append((0, 256, T["kwin"][b:b + 64, wbase:wbase + 256]))
                pieces.append((256, 384, QKVB[b:b + 64, ck * 512: ck * 512 + 128]))
            elif qc == 1:
                pieces.append((0, 128, T["kwin"][b:b + 64, wbase + 128:wbase + 256]))
                pieces.append((128, 384, QKVB[b:b + 64, ck * 512: ck * 512 + 256]))
            elif qc == 2:
                pieces.append((0, 384, QKVB[b:b + 64, ck * 512: ck * 512 + 384]))
            else:
                pieces.append((0, 384, QKVB[b:b + 64, ck * 512 + 128: ck * 512 + 512]))
            p.t.wait(p.s, p.state.get(("bS_free", sb_), 0))
            p.t.wait(p.v, p.state.get(("bS_freev", sb_), 0))

            def t_sc(t, lh=lh, pieces=pieces, sb_=sb_):
                last = None
                for (c0, c1, src) in pieces:
                    last = t.matmul(BK[sb_][:, c0:c1], lh, src, start=True, stop=True)
                return last

            p.t.mark(t_sc)
            if n == 0 and qc == 0:
                msk = T["mask_f0"]
            elif n == 0 and qc == 1:
                msk = T["mask_f1"]
            else:
                msk = T["mask_std"]
            p.v.wait(p.t)
            p.v.mark(lambda v, b_=BK[sb_], m=msk: v.tensor_add(out=b_[:, 0:384],
                                                               in0=b_[:, 0:384], in1=m))
            p.state[("bS_freev", sb_)] = p.v.n
            p.s.wait(p.v)
            p.s.wait(p.t, p.state.get(("bPBt", h % 2), 0))
            pb = PB2[:, ((h % 2) * 4 + qc) * 384:((h % 2) * 4 + qc + 1) * 384]
            c2 = cols[:, cb + qc:cb + qc + 1]
            p.s.mark(lambda s, o=pb, b_=BK[sb_], c2=c2: s.activation(
                out=o, in_=b_[:, 0:384], func=AF.Exp, accum_out=c2))
            p.state[("bS_free", sb_)] = p.s.n
        p.v.wait(p.s)
        p.v.mark(lambda v: v.reciprocal(out=cols[:, cb + 4:cb + 8], in_=cols[:, cb:cb + 4]))
        for qc in range(4):
            pb = PB2[:, ((h % 2) * 4 + qc) * 384:((h % 2) * 4 + qc + 1) * 384]
            p.v.mark(lambda v, o=pb, c3=cols[:, cb + 4 + qc:cb + 5 + qc]: v.tensor_scalar(
                out=o, in0=o, scalar1=c3, scalar2=None, op0=ALU.mult))
        p.state[("bnorms", h)] = p.v.n

    def tail_b(h):
        b = (h % 2) * 64
        cq = h // 2
        p.t.wait(p.v, p.state[("bnorms", h)])
        p.t.wait(p.s, p.state[("vt", h)])
        p.t.wait(p.s, p.state.get(("bAV_free",), 0))
        evac_marks = {}
        for qc in range(5):
            if qc < 4:
                tb = PTB2[:, (qc % 2) * 1024:(qc % 2) * 1024 + 384]
                p.t.wait(p.s, p.state.get(("bPT_free", qc % 2), 0))

                def t_pt3(t, h=h, qc=qc, tb=tb):
                    last = None
                    for j in range(3):
                        last = t.transpose(tb[:, j * 128:(j + 1) * 128],
                                           PB2[:, ((h % 2) * 4 + qc) * 384 + j * 128:
                                               ((h % 2) * 4 + qc) * 384 + (j + 1) * 128],
                                           T["ident"])
                    return last

                p.t.mark(t_pt3)
                p.s.wait(p.t)
                p.s.mark(lambda s, o=PT2[:, qc * 384:(qc + 1) * 384], tb=tb:
                         s.copy(out=o, in_=tb))
                p.state[("bPT_free", qc % 2)] = p.s.n
                evac_marks[qc] = p.s.n
            if qc >= 1:
                qp = qc - 1
                p.t.wait(p.s, evac_marks[qp])

                def t_av3(t, b=b, h=h, qp=qp):
                    last = None
                    for j in range(3):
                        kc = qp + j
                        last = t.matmul(BK[AV_BANK][b:b + 64, qp * 128:(qp + 1) * 128],
                                        VT2[:, (h % 2) * 384 + kc * 64:(h % 2) * 384 + (kc + 1) * 64],
                                        PT2[:, qp * 384 + j * 128: qp * 384 + (j + 1) * 128],
                                        start=(j == 0), stop=(j == 2), tile_position=(0, b))
                    return last

                p.t.mark(t_av3)
        p.state[("bPBt", h % 2)] = p.t.n
        p.s.wait(p.t)
        abdst = ABT[b:b + 64, :].rearrange("p (cc l) -> p cc l", cc=8)[:, cq:cq + 1, :]
        avsrc = BK[AV_BANK][b:b + 64, 0:512].rearrange("p (cc l) -> p cc l", cc=1)
        p.s.mark(lambda s, o=abdst, i=avsrc: s.copy(out=o, in_=i))
        p.state[("bAV_free",)] = p.s.n

    for h in range(H):
        scores_b(h)
        if h >= 1:
            tail_b(h - 1)
    tail_b(H - 1)
    ab_ready_s = p.s.n
    p.s.wait(p.t)
    for c in range(8):
        ksrc = QKVB[:, (8 + c) * 512 + 256:(8 + c) * 512 + 512]
        vsrc = QKVB[:, (16 + c) * 512 + 256:(16 + c) * 512 + 512]
        kdst = T["kwin"][:, seq * 2048 + c * 256: seq * 2048 + (c + 1) * 256]
        vdst = T["vwin"][:, seq * 2048 + c * 256: seq * 2048 + (c + 1) * 256]
        p.s.emit(lambda s, o=kdst, i=ksrc: s.copy(out=o, in_=i))
        p.s.mark(lambda s, o=vdst, i=vsrc: s.copy(out=o, in_=i))
    p.state["win_update"] = (p.s, p.s.n)
    for c in range(8):
        sl = c % 3
        if c >= 3:
            p.y.wait(p.t, p.state[("bwo_pe", sl)])
        dst = T["wbuf_2"][:, sl * 1024:(sl + 1) * 1024]
        ld = p.dma("y", ("wout", sl), dst, T["s_wout"][c])
        p.t.wait(p.stream(("wout", sl)), ld)
        if c == 0:
            p.t.wait(p.s, ab_ready_s)
            p.t.wait(p.v)

        def t_wo(t, c=c, sl=sl):
            last = None
            for lc in range(4):
                for dhf in range(2):
                    last = t.matmul(BK[lc * 2 + dhf],
                                    ABT[:, c * 512 + lc * 128: c * 512 + (lc + 1) * 128],
                                    T["wbuf_2"][:, sl * 1024 + dhf * 512: sl * 1024 + (dhf + 1) * 512],
                                    start=(c == 0), stop=(c == 7))
            return last

        p.t.mark(t_wo)
        p.state[("bwo_pe", sl)] = p.t.n
    p.v.wait(p.t)
    for lc in range(4):
        xs = T["X"][:, seq * 4096 + lc * D: seq * 4096 + (lc + 1) * D]
        b = T["P8"][:, lc * 1024:(lc + 1) * 1024]
        p.v.mark(lambda v, xs=xs, b=b: v.tensor_add(out=xs, in0=xs, in1=b))
    p.state[("X", seq)] = (p.v, p.v.n)
    p.state["xlf_free"] = [(p.t, p.t.n)]
    p.state["banks_free"] = (p.v, p.v.n)


def store_segment(nc, p, T, seq, n):
    xq, xv = p.state[("X", seq)]
    p.g.wait(xq, xv)
    xs = T["X"][:, seq * 4096:(seq + 1) * 4096].rearrange("p (lc d) -> p lc d", lc=4)
    dst = T["yout"][seq, n * SEG:(n + 1) * SEG, :].rearrange("(lc q) d -> q lc d", q=128)
    st = p.dma("g", ("x", seq), dst, xs)
    p.state[("Xstore", seq)] = st
    if n + 1 < _NSEG_RUN:
        p.g.wait(p.stream(("x", seq)), st)
        src2 = T["xin"][seq, (n + 1) * SEG:(n + 2) * SEG, :].rearrange("(lc q) d -> q lc d", q=128)
        ld = p.dma("g", ("x", seq), xs, src2)
        p.state[("X", seq)] = (p.stream(("x", seq)), ld)


def epilogue(nc, p, T):
    for key, ctr in p.dstream.items():
        p.g.wait(ctr, ctr.n)



def kernel(L0_attn_a_x, L0_attn_a_Wq, L0_attn_a_Wk, L0_attn_a_Wv, L0_attn_a_Wo,
           L0_attn_a_betas, L0_attn_b_w_qkv, L0_attn_b_w_out, L0_ffn_w_gate,
           L0_ffn_w_up, L0_ffn_w_down):
    x = np.ascontiguousarray(L0_attn_a_x)
    nc = build_program()
    in_maps = []
    for c in range(8):
        in_maps.append({
            "x": np.ascontiguousarray(x[2 * c:2 * c + 2]),
            "wq": np.ascontiguousarray(L0_attn_a_Wq),
            "wk": np.ascontiguousarray(L0_attn_a_Wk),
            "wv": np.ascontiguousarray(L0_attn_a_Wv),
            "wo": np.ascontiguousarray(L0_attn_a_Wo),
            "betas": np.ascontiguousarray(L0_attn_a_betas),
            "wqkv": np.ascontiguousarray(L0_attn_b_w_qkv),
            "wout": np.ascontiguousarray(L0_attn_b_w_out),
            "wg": np.ascontiguousarray(L0_ffn_w_gate),
            "wu": np.ascontiguousarray(L0_ffn_w_up),
            "wd": np.ascontiguousarray(L0_ffn_w_down),
        })
    res = run_bass_kernel_spmd(nc, in_maps, core_ids=list(range(8)))
    return np.concatenate([r["y"] for r in res.results], axis=0)


# version 14
# speedup vs baseline: 1.2039x; 1.0261x over previous
import numpy as np
from contextlib import ExitStack
import concourse.bass as bass
import concourse.mybir as mybir
from concourse.bass_utils import run_bass_kernel_spmd

F32 = mybir.dt.float32
BF16 = mybir.dt.bfloat16
I32 = mybir.dt.int32
AF = mybir.ActivationFunctionType
ALU = mybir.AluOpType
AX = mybir.AxisListType

D = 1024
SEG = 512
NSEG = 8
NSEQ = 2
H = 16
DH = 64
HID = 3072
WIN = 256
QSCALE = 1.0 / 8.0
DQ = 1.0 / 127.0
EPS = 1e-5

_NSEG_RUN = NSEG
_SUBLAYERS = (1, 2, 3, 4)
_AA_LIMIT = 99
_DEBUG_SHAPE = None
_DEBUG_EMIT = None


class EngQ:

    def __init__(self, name, sem, self_chain=False):
        self.name = name
        self.sem = sem
        self.n = 0
        self.q = []
        self.self_chain = self_chain

    def emit(self, fn):
        if self.self_chain:
            return self.mark(fn)
        self.q.append(fn)

    def mark(self, fn, inc=1):
        if self.self_chain and self.n > 0:
            sem0, tgt = self.sem, self.n
            self.q.append(lambda e, sem=sem0, val=tgt: e.wait_ge(sem, val))
        self.n += inc
        sem, k = self.sem, inc
        self.q.append(lambda e, fn=fn, sem=sem, k=k: fn(e).then_inc(sem, k))
        return self.n

    def emit_raw(self, fn):
        self.q.append(fn)

    def wait(self, other, val=None):
        if val is None:
            val = other.n
        if val <= 0:
            return
        sem = other.sem
        self.q.append(lambda e, sem=sem, val=val: e.wait_ge(sem, val))


class Prog:
    def __init__(self, nc, sems, mksem):
        self.nc = nc
        self.t = EngQ("tensor", sems["pe"])
        self.v = EngQ("vector", sems["dve"], self_chain=True)
        self.s = EngQ("scalar", sems["act"], self_chain=True)
        self.g = EngQ("gpsimd", sems["pool"], self_chain=True)
        self.y = EngQ("sync", sems["dma_w"])
        self.state = {}
        self._mksem = mksem
        self.dstream = {}

    def stream(self, key):
        if key not in self.dstream:
            self.dstream[key] = EngQ(f"dma_{key}", self._mksem(f"dma_{key}"))
        return self.dstream[key]

    def dma(self, q, stream, out, in_, **kw):
        eq = {"y": self.y, "g": self.g, "s": self.s}[q]
        ctr = self.stream(stream)
        if eq.self_chain and eq.n > 0:
            sem0, tgt = eq.sem, eq.n
            eq.q.append(lambda e, sem=sem0, val=tgt: e.wait_ge(sem, val))
        ctr.n += 16
        sem = ctr.sem
        eq.q.append(lambda e, out=out, in_=in_, sem=sem, kw=kw:
                    e.dma_start(out=out, in_=in_, **kw).then_inc(sem, 16))
        return ctr.n

    def barrier(self, consumers, producers):
        for c in consumers:
            for pr in producers:
                if pr.n > 0:
                    c.wait(pr)


def build_program():
    nc = bass.Bass("TRN2", target_bir_lowering=False)
    T = {}

    T["xin"] = nc.dram_tensor("x", [NSEQ, NSEG * SEG, D], F32, kind="ExternalInput").ap()
    T["wq_d"] = nc.dram_tensor("wq", [D, D], F32, kind="ExternalInput").ap()
    T["wk_d"] = nc.dram_tensor("wk", [D, D], F32, kind="ExternalInput").ap()
    T["wv_d"] = nc.dram_tensor("wv", [D, D], F32, kind="ExternalInput").ap()
    T["wo_d"] = nc.dram_tensor("wo", [D, D], F32, kind="ExternalInput").ap()
    T["betas_d"] = nc.dram_tensor("betas", [1, H, 1, DH], F32, kind="ExternalInput").ap()
    T["wqkv_d"] = nc.dram_tensor("wqkv", [D, 3 * D], F32, kind="ExternalInput").ap()
    T["wout_d"] = nc.dram_tensor("wout", [D, D], F32, kind="ExternalInput").ap()
    T["wg_d"] = nc.dram_tensor("wg", [HID, D], I32, kind="ExternalInput").ap()
    T["wu_d"] = nc.dram_tensor("wu", [HID, D], I32, kind="ExternalInput").ap()
    T["wd_d"] = nc.dram_tensor("wd", [D, HID], I32, kind="ExternalInput").ap()
    T["yout"] = nc.dram_tensor("y", [NSEQ, NSEG * SEG, D], F32, kind="ExternalOutput").ap()
    if _DEBUG_SHAPE is not None:
        T["dbg"] = nc.dram_tensor("dbg", list(_DEBUG_SHAPE), F32, kind="ExternalOutput").ap()

    T["s_wqa"] = nc.dram_tensor("s_wqa", [H, 8, 128, DH], BF16, kind="Internal").ap()
    T["s_wka"] = nc.dram_tensor("s_wka", [H, 8, 128, DH], BF16, kind="Internal").ap()
    T["s_wva"] = nc.dram_tensor("s_wva", [H, 8, 128, DH], BF16, kind="Internal").ap()
    T["s_wo"] = nc.dram_tensor("s_wo", [16, 64, D], BF16, kind="Internal").ap()
    T["s_wqkv"] = nc.dram_tensor("s_wqkv", [24, 8, 128, 128], BF16, kind="Internal").ap()
    T["s_wout"] = nc.dram_tensor("s_wout", [8, 128, D], BF16, kind="Internal").ap()
    T["s_wg"] = nc.dram_tensor("s_wg", [24, 8, 128, 128], BF16, kind="Internal").ap()
    T["s_wu"] = nc.dram_tensor("s_wu", [24, 8, 128, 128], BF16, kind="Internal").ap()
    T["s_wd"] = nc.dram_tensor("s_wd", [24, 8, 128, 128], BF16, kind="Internal").ap()
    T["s_zc"] = nc.dram_tensor("s_zc", [DH, H], F32, kind="Internal").ap()

    with ExitStack() as ctx:
        def sb(name, shape, dt):
            return ctx.enter_context(nc.sbuf_tensor(name, shape, dt)).ap()

        def psv(name, shape, dt=F32):
            return ctx.enter_context(nc.psum_tensor(name, shape, dt)).ap()

        T["X"] = sb("X", [128, NSEQ * 4 * D], F32)
        T["kwin"] = sb("kwin", [128, NSEQ * 8 * WIN], BF16)
        T["vwin"] = sb("vwin", [128, NSEQ * 8 * WIN], BF16)
        T["mem_sb"] = sb("mem_sb", [128, NSEQ * H * DH], F32)
        T["membf"] = sb("membf", [128, DH], BF16)
        T["zc_sb"] = sb("zc_sb", [128, NSEQ * H], F32)
        T["invg"] = sb("invg", [128, H], F32)
        T["grow"] = sb("grow", [1, D], F32)
        T["ident"] = sb("ident", [128, 128], BF16)
        T["ident64"] = sb("ident64", [128, 64], BF16)
        T["ident64a"] = sb("ident64a", [128, 64], BF16)
        T["ident64f"] = sb("ident64f", [128, 64], F32)
        T["onesc"] = sb("onesc", [128, 64], BF16)
        T["onesf"] = sb("onesf", [1, 1], F32)
        T["mask_std"] = sb("mask_std", [128, 384], BF16)
        T["mask_f0"] = sb("mask_f0", [128, 384], BF16)
        T["mask_f1"] = sb("mask_f1", [128, 384], BF16)
        T["c12col"] = sb("c12col", [128, 1], F32)
        T["lncols"] = sb("lncols", [128, 32], F32)
        T["epscol"] = sb("epscol", [128, 1], F32)
        T["cols"] = sb("cols", [128, 32], F32)
        T["rows"] = sb("rows", [128, 1536], F32)
        T["rows_bf"] = sb("rows_bf", [1, 1024], BF16)
        T["TSB"] = sb("TSB", [128, 256], F32)
        T["ESC"] = sb("ESC", [128, 2048], BF16)
        T["BSB"] = sb("BSB", [128, 2 * 256], F32)
        T["SSB"] = sb("SSB", [128, 2 * 512], BF16)

        T["wbuf_a"] = sb("wbuf_a", [128, 2 * 3 * 512], BF16)
        T["wo_buf"] = sb("wo_buf", [128, 2 * D], BF16)
        T["wbuf_1"] = sb("wbuf_1", [128, 3 * 2048], BF16)
        T["wbuf_2"] = sb("wbuf_2", [128, 3 * 1024], BF16)

        AR = 99328
        arena = ctx.enter_context(nc.sbuf_tensor("arena", [128, AR], mybir.dt.uint8)).ap()

        def carve(off, shape, dt):
            nbytes = int(np.prod(shape[1:])) * mybir.dt.size(dt)
            assert off + nbytes <= AR, (off, nbytes, AR)
            return arena[:, off:off + nbytes].bitcast(dt)

        T["xlt"] = carve(0, [128, 2 * D], BF16)
        T["xlf"] = carve(4096, [128, 8 * SEG], BF16)
        T["QA"] = carve(12288, [128, H * SEG], BF16)
        T["KA"] = carve(28672, [128, H * SEG], BF16)
        T["VA"] = carve(45056, [128, H * SEG], BF16)
        T["SQA"] = carve(61440, [128, H * SEG], BF16)
        T["A2"] = carve(77824, [128, 8 * SEG], BF16)
        T["PB"] = carve(86016, [128, 8 * SEG], BF16)
        T["PT"] = carve(94208, [128, 4 * SEG], BF16)
        T["VTST"] = carve(98304, [128, 512], BF16)
        T["xlf2"] = carve(12288, [128, 8 * SEG], BF16)
        T["HA"] = carve(20480, [128, 24 * SEG], BF16)
        T["HB"] = carve(45056, [128, 24 * SEG], BF16)
        T["QKVB"] = carve(12288, [128, 24 * SEG], BF16)
        T["ABT"] = carve(36864, [128, 8 * SEG], BF16)
        T["PB2"] = carve(45056, [128, 8 * 384], BF16)
        T["PT2"] = carve(51200, [128, 4 * 384], BF16)
        T["VT2"] = carve(54272, [128, 2 * 6 * DH], BF16)
        T["cv_i"] = carve(12288, [128, 3 * D], I32)
        T["cv_f"] = carve(24576, [128, 3 * D], F32)
        T["cv_b"] = carve(36864, [128, 3 * D], BF16)
        T["cv_t"] = carve(43008, [128, 1024], BF16)

        P8 = psv("P8", [128, 4096], F32)
        T["P8"] = P8
        T["BK"] = [P8[:, i * 512:(i + 1) * 512] for i in range(8)]

        sems = {}
        for name in ["pe", "dve", "act", "pool", "dma_w"]:
            sems[name] = ctx.enter_context(nc.semaphore(name))
        mksem = lambda nm: ctx.enter_context(nc.semaphore(nm))
        p = Prog(nc, sems, mksem)

        build_phases(nc, p, T)

        with nc.Block() as block:
            @block.tensor
            def _(e):
                for f in p.t.q:
                    f(e)

            @block.vector
            def _(e):
                for f in p.v.q:
                    f(e)

            @block.scalar
            def _(e):
                for f in p.s.q:
                    f(e)

            @block.gpsimd
            def _(e):
                for f in p.g.q:
                    f(e)

            @block.sync
            def _(e):
                for f in p.y.q:
                    f(e)

    return nc



def build_phases(nc, p, T):
    prologue(nc, p, T)
    for n in range(_NSEG_RUN):
        if 1 in _SUBLAYERS:
            for s in range(NSEQ):
                ln_into_xlf(nc, p, T, seq=s, xlf=T["xlf"])
                attn_a(nc, p, T, seq=s, n=n)
        if 2 in _SUBLAYERS:
            ffn(nc, p, T)
        if 3 in _SUBLAYERS:
            for s in range(NSEQ):
                ln_into_xlf(nc, p, T, seq=s, xlf=T["xlf"])
                attn_b(nc, p, T, seq=s, n=n)
        if 4 in _SUBLAYERS:
            ffn(nc, p, T)
        for s in range(NSEQ):
            store_segment(nc, p, T, seq=s, n=n)
    if _DEBUG_EMIT is not None:
        _DEBUG_EMIT(nc, p, T)
    epilogue(nc, p, T)


def prologue(nc, p, T):
    BK = T["BK"]
    ident, ident64, ident64a, onesc = T["ident"], T["ident64"], T["ident64a"], T["onesc"]

    def asel(out, in_, cmp, fill, base, pattern, chm):
        p.g.emit(lambda g, out=out, in_=in_, cmp=cmp, fill=fill, base=base,
                 pattern=pattern, chm=chm: g.affine_select(
                     out=out, in_=in_, compare_op=cmp, fill=fill, base=base,
                     pattern=pattern, channel_multiplier=chm))

    def gmemset(ap_, val):
        p.g.emit(lambda g, ap_=ap_, val=val: g.memset(ap_, val))

    gmemset(ident, 0.0)
    asel(ident, ident, ALU.not_equal, 1.0, 0, [[-1, 128]], 1)
    gmemset(ident64, 0.0)
    asel(ident64, ident64, ALU.not_equal, 1.0, -64, [[-1, 64]], 1)
    gmemset(ident64a, 0.0)
    asel(ident64a, ident64a, ALU.not_equal, 1.0, 0, [[-1, 64]], 1)
    gmemset(T["ident64f"], 0.0)
    asel(T["ident64f"], T["ident64f"], ALU.not_equal, 1.0, -64, [[-1, 64]], 1)
    gmemset(onesc, 1.0)
    for m in (T["mask_std"], T["mask_f0"], T["mask_f1"]):
        gmemset(m, -12.0)
        asel(m, m, ALU.is_ge, -30012.0, -1, [[1, 384]], -1)
        asel(m, m, ALU.is_ge, -30012.0, 256, [[-1, 384]], 1)
    asel(T["mask_f0"], T["mask_f0"], ALU.is_ge, -30012.0, -256, [[1, 384]], 0)
    asel(T["mask_f1"], T["mask_f1"], ALU.is_ge, -30012.0, -128, [[1, 384]], 0)
    gmemset(T["c12col"], -12.0)
    gmemset(T["epscol"], EPS)
    gmemset(T["mem_sb"], 0.0)
    gmemset(T["zc_sb"], 0.0)
    gmemset(T["kwin"], 0.0)
    gmemset(T["vwin"], 0.0)
    gmemset(T["onesf"], 1.0)

    for s in range(NSEQ):
        dst = T["X"][:, s * 4096:(s + 1) * 4096].rearrange("p (lc d) -> p lc d", lc=4)
        src = T["xin"][s, 0:SEG, :].rearrange("(lc q) d -> q lc d", q=128)
        ld = p.dma("g", ("x", s), dst, src)
        p.state[("X", s)] = (p.stream(("x", s)), ld)

    p.dma("s", "misc", T["rows"][0:1, 0:1024], T["betas_d"].rearrange("a h b d -> a (h b d)"))
    p.s.wait(p.stream("misc"))
    p.s.mark(lambda s: s.activation(out=T["grow"], in_=T["rows"][0:1, 0:1024],
                                    func=AF.Sigmoid))
    p.t.wait(p.s)
    p.t.wait(p.g)

    def t_invg(t):
        last = None
        for hp in range(2):
            for h in range(H):
                last = t.matmul(BK[0][hp * 64:hp * 64 + 64, h:h + 1],
                                T["grow"][0:1, h * 64:(h + 1) * 64], T["onesf"],
                                start=True, stop=True,
                                tile_position=(0, hp * 64))
        return last

    p.t.mark(t_invg)
    p.v.wait(p.t)
    p.v.mark(lambda v: v.tensor_scalar(out=T["invg"], in0=BK[0][:, 0:H],
                                       scalar1=-1.0, scalar2=1.0,
                                       op0=ALU.mult, op1=ALU.add))
    bank_free = p.v.n

    cv_f, cv_b, cv_i, cv_t = T["cv_f"], T["cv_b"], T["cv_i"], T["cv_t"]
    halfF = 1536
    halfB = 1536

    conv_marks = []

    def conv_f32(src_rows, n_rows_chunks, width, store_fn, qscale_cols=None):
        for k in range(n_rows_chunks):
            sl = (k % 2) if width <= halfF else 0
            fa = cv_f[:, sl * halfF:sl * halfF + width]
            ba = cv_b[:, sl * halfB:sl * halfB + width]
            p.y.wait(p.stream(("cvS", sl)))
            p.y.wait(p.v)
            p.y.wait(p.s)
            if width > halfF:
                p.y.wait(p.stream(("cvS", 1 - sl)))
            ld = p.dma("y", ("cvL", sl), fa, src_rows(k))
            p.v.wait(p.stream(("cvL", sl)), ld)
            if qscale_cols is None:
                p.v.mark(lambda v, ba=ba, fa=fa: v.tensor_copy(out=ba, in_=fa))
            else:
                c0, c1 = qscale_cols
                if c0 > 0:
                    p.v.emit(lambda v, o=ba[:, 0:c0], i=fa[:, 0:c0]: v.tensor_copy(out=o, in_=i))
                p.v.emit(lambda v, o=ba[:, c0:c1], i=fa[:, c0:c1]: v.tensor_scalar(
                    out=o, in0=i, scalar1=QSCALE, scalar2=None, op0=ALU.mult))
                if c1 < width:
                    p.v.mark(lambda v, o=ba[:, c1:width], i=fa[:, c1:width]: v.tensor_copy(out=o, in_=i))
                else:
                    p.v.mark(lambda v: v.engine_nop())
            store_fn(k, sl, ba, p.v.n)

    for w_d, s_w, qs in ((T["wq_d"], T["s_wqa"], (0, D)), (T["wk_d"], T["s_wka"], None),
                         (T["wv_d"], T["s_wva"], None)):
        def store(k, sl, ba, vmark, s_w=s_w):
            out = s_w[:, k, :, :].transpose([1, 0, 2])
            in_ = ba.rearrange("p (g j) -> p g j", g=16)
            p.s.wait(p.v, vmark)
            p.dma("s", ("cvS", sl), out, in_)

        conv_f32(lambda k, w_d=w_d: w_d[k * 128:(k + 1) * 128, :], 8, D, store,
                 qscale_cols=qs)

    def store_wo(k, sl, ba, vmark):
        p.s.wait(p.v, vmark)
        p.dma("s", ("cvS", sl), T["s_wo"][2 * k], ba[0:64, :])
        p.dma("s", ("cvS", sl), T["s_wo"][2 * k + 1], ba[64:128, :])

    conv_f32(lambda k: T["wo_d"][k * 128:(k + 1) * 128, :], 8, D, store_wo)
    p.state["scr_a"] = [(c, c.n) for c in p.dstream.values()] + [(p.s, p.s.n), (p.g, p.g.n)]

    def store_wqkv(k, sl, ba, vmark):
        out = T["s_wqkv"][:, k, :, :].transpose([1, 0, 2])
        in_ = ba.rearrange("p (c j) -> p c j", c=24)
        p.s.wait(p.v, vmark)
        p.dma("s", ("cvS", sl), out, in_)

    conv_f32(lambda k: T["wqkv_d"][k * 128:(k + 1) * 128, :], 8, 3 * D, store_wqkv,
             qscale_cols=(0, D))

    def store_wout(k, sl, ba, vmark):
        p.s.wait(p.v, vmark)
        p.dma("s", ("cvS", sl), T["s_wout"][k], ba)

    conv_f32(lambda k: T["wout_d"][k * 128:(k + 1) * 128, :], 8, D, store_wout)
    p.state["scr_b"] = [(c, c.n) for c in p.dstream.values()] + [(p.s, p.s.n), (p.g, p.g.n)]

    p.t.wait(p.v, bank_free)

    def conv_int(src_rows, n_chunks, width, store_fn):
        for k in range(n_chunks):
            sl = (k % 2) if width <= halfF else 0
            ia = cv_i[:, sl * halfF:sl * halfF + width]
            fa = cv_f[:, sl * halfF:sl * halfF + width]
            ba = cv_b[:, sl * halfB:sl * halfB + width]
            p.y.wait(p.t)
            p.y.wait(p.v)
            p.y.wait(p.s)
            if width > halfF:
                p.y.wait(p.stream(("cvS", 1 - sl)))
            p.y.wait(p.stream(("cvS", sl)))
            ld = p.dma("y", ("cvL", sl), ia, src_rows(k))
            p.v.wait(p.stream(("cvL", sl)), ld)
            p.v.mark(lambda v, fa=fa, ia=ia: v.tensor_copy(out=fa, in_=ia))
            p.s.wait(p.v)
            p.s.mark(lambda s, ba=ba, fa=fa: s.activation(out=ba, in_=fa,
                                                          func=AF.Copy, scale=DQ))
            store_fn(k, ba, p.s.n)

    for w_d, s_w in ((T["wg_d"], T["s_wg"]), (T["wu_d"], T["s_wu"])):
        def store_g(r, ba, smark, s_w=s_w):
            p.t.wait(p.s, smark)
            p.t.wait(p.stream("cvt"))

            def t_fn(t, ba=ba):
                last = None
                for k in range(8):
                    last = t.transpose(BK[0].bitcast(BF16)[:, k * 128:(k + 1) * 128],
                                       ba[:, k * 128:(k + 1) * 128], T["ident"])
                return last

            p.t.mark(t_fn)
            p.v.wait(p.t)
            p.v.mark(lambda v: v.tensor_copy(out=cv_t, in_=BK[0].bitcast(BF16)[:, 0:1024]))
            p.g.wait(p.v)
            out = s_w[r].transpose([1, 0, 2])
            p.dma("g", "cvt", out, cv_t.rearrange("p (k j) -> p k j", k=8))

        conv_int(lambda r, w_d=w_d: w_d[r * 128:(r + 1) * 128, :], 24, D, store_g)

    def store_wd(r, ba, smark):
        p.t.wait(p.s, smark)
        for half in range(3):
            p.t.wait(p.stream("cvt"))

            def t_fn(t, ba=ba, half=half):
                last = None
                for kk in range(8):
                    c = half * 8 + kk
                    last = t.transpose(BK[0].bitcast(BF16)[:, kk * 128:(kk + 1) * 128],
                                       ba[:, c * 128:(c + 1) * 128], T["ident"])
                return last

            p.t.mark(t_fn)
            p.v.wait(p.t)
            p.v.mark(lambda v: v.tensor_copy(out=cv_t, in_=BK[0].bitcast(BF16)[:, 0:1024]))
            p.g.wait(p.v)
            for kk in range(8):
                c = half * 8 + kk
                p.dma("g", "cvt", T["s_wd"][c, r], cv_t.rearrange("p (k j) -> p k j", k=8)[:, kk, :])

    conv_int(lambda r: T["wd_d"][r * 128:(r + 1) * 128, :], 8, HID, store_wd)

    p.state["scratch"] = ([(c, c.n) for c in p.dstream.values()] +
                          [(p.g, p.g.n), (p.v, p.v.n), (p.s, p.s.n), (p.t, p.t.n)])


def wait_scratch(p, eng, key="scratch"):
    for q, val in p.state[key]:
        eng.wait(q, val)


def ln_into_xlf(nc, p, T, seq, xlf):
    X, xlt, lnc, BK = T["X"], T["xlt"], T["lncols"], T["BK"]
    xq = p.state.get(("X", seq), (None, 0))
    if xq[0] is not None:
        p.v.wait(xq[0], xq[1])
        p.s.wait(xq[0], xq[1])
    p.s.wait(p.t)
    X4 = X[:, seq * 4096:(seq + 1) * 4096]
    p.v.mark(lambda v: v.reduce_sum(out=lnc[:, 0:4],
                                    in_=X4.rearrange("p (lc d) -> p lc d", lc=4),
                                    axis=AX.X))
    for lc in range(4):
        Xt = X4[:, lc * D:(lc + 1) * D]
        xs = xlt[:, (lc % 2) * D:(lc % 2 + 1) * D]
        p.s.mark(lambda s, Xt=Xt, xs=xs, c=lnc[:, 4 + lc:5 + lc]: s.activation(
            out=xs, in_=Xt, func=AF.Square, accum_out=c))
    p.v.wait(p.s)
    p.v.emit(lambda v: v.tensor_scalar(out=lnc[:, 8:12], in0=lnc[:, 0:4],
                                       scalar1=1.0 / D, scalar2=None, op0=ALU.mult))
    p.v.emit(lambda v: v.tensor_mul(out=lnc[:, 12:16], in0=lnc[:, 8:12], in1=lnc[:, 8:12]))
    p.v.mark(lambda v: v.scalar_tensor_tensor(out=lnc[:, 16:20], in0=lnc[:, 4:8],
                                              scalar=1.0 / D, in1=lnc[:, 12:16],
                                              op0=ALU.mult, op1=ALU.subtract))
    p.s.wait(p.v)
    p.s.mark(lambda s: s.activation(out=lnc[:, 20:24], in_=lnc[:, 16:20],
                                    func=AF.Sqrt, bias=T["epscol"]))
    p.v.wait(p.s)
    p.v.emit(lambda v: v.reciprocal(out=lnc[:, 24:28], in_=lnc[:, 20:24]))
    p.v.mark(lambda v: v.scalar_tensor_tensor(out=lnc[:, 28:32], in0=lnc[:, 8:12],
                                              scalar=-1.0, in1=lnc[:, 24:28],
                                              op0=ALU.mult, op1=ALU.mult))
    for lc in range(4):
        Xt = X4[:, lc * D:(lc + 1) * D]
        xs = xlt[:, (lc % 2) * D:(lc % 2 + 1) * D]
        if lc >= 2:
            p.v.wait(p.t)
        p.v.mark(lambda v, Xt=Xt, xs=xs, c6=lnc[:, 24 + lc:25 + lc],
                 c7=lnc[:, 28 + lc:29 + lc]: v.tensor_scalar(
                     out=xs, in0=Xt, scalar1=c6, scalar2=c7,
                     op0=ALU.mult, op1=ALU.add))
        bank = BK[6 + (lc % 2)]
        p.t.wait(p.v)
        p.t.wait(p.v, p.state.get(("lnbank", 6 + lc % 2), 0))

        def t_fn(t, xs=xs, bank=bank):
            last = None
            for cc in range(8):
                last = t.transpose(bank.bitcast(BF16)[:, cc * 128:(cc + 1) * 128],
                                   xs[:, cc * 128:(cc + 1) * 128], T["ident"])
            return last

        p.t.mark(t_fn)
        p.v.wait(p.t)
        out = xlf.rearrange("p (cc l) -> p cc l", cc=8)[:, :, lc * 128:(lc + 1) * 128]
        p.v.mark(lambda v, out=out, bank=bank: v.tensor_copy(
            out=out, in_=bank.bitcast(BF16)[:, 0:1024].rearrange("p (cc l) -> p cc l", cc=8)))
        p.state[("lnbank", 6 + lc % 2)] = p.v.n
    p.state["xlf"] = (p.v, p.v.n)


def attn_a(nc, p, T, seq, n):
    BK = T["BK"]
    QA, KA, VA, SQA, A2 = T["QA"], T["KA"], T["VA"], T["SQA"], T["A2"]
    PB, PT = T["PB"], T["PT"]
    rows, cols = T["rows"], T["cols"]
    wbuf, wo_buf = T["wbuf_a"], T["wo_buf"]
    wait_scratch(p, p.y)
    p.y.wait(p.t)
    p.t.wait(p.g)
    p.v.wait(p.t)
    p.s.wait(p.t)
    xq, xv = p.state["xlf"]
    p.t.wait(xq, xv)

    ev_marks = {}
    for g in range(16):
        sl = g % 2
        b0, b1 = BK[sl * 2], BK[sl * 2 + 1]
        if g >= 2:
            pe_done = p.state[("aproj_pe", sl)]
            p.y.wait(p.t, pe_done)
        for idx, s_w in enumerate((T["s_wqa"], T["s_wka"], T["s_wva"])):
            dst = wbuf[:, sl * 1536 + idx * 512: sl * 1536 + idx * 512 + 512]
            dst = dst.rearrange("p (k j) -> p k j", k=8)
            ld = p.dma("y", ("wA", sl), dst, s_w[g].transpose([1, 0, 2]))
        p.t.wait(p.stream(("wA", sl)), ld)
        if g >= 2:
            for q, val in ev_marks[sl]:
                p.t.wait(q, val)

        def t_fn(t, sl=sl, b0=b0, b1=b1):
            last = None
            for k in range(8):
                last = t.matmul(
                    b0[64:128, :],
                    T["wbuf_a"][:, sl * 1536 + k * 64: sl * 1536 + (k + 1) * 64],
                    T["xlf"][:, k * 512:(k + 1) * 512],
                    start=(k == 0), stop=(k == 7), tile_position=(0, 64))
                last = t.matmul(
                    b0[0:64, :],
                    T["wbuf_a"][:, sl * 1536 + 1024 + k * 64: sl * 1536 + 1024 + (k + 1) * 64],
                    T["xlf"][:, k * 512:(k + 1) * 512],
                    start=(k == 0), stop=(k == 7), tile_position=(0, 0))
            for k in range(8):
                last = t.matmul(
                    b1[64:128, :],
                    T["wbuf_a"][:, sl * 1536 + 512 + k * 64: sl * 1536 + 512 + (k + 1) * 64],
                    T["xlf"][:, k * 512:(k + 1) * 512],
                    start=(k == 0), stop=(k == 7), tile_position=(0, 64))
            return last

        p.t.mark(t_fn)
        p.state[("aproj_pe", sl)] = p.t.n
        sp = (g % 2) * 256 + (g // 2) * 32
        qdst = QA[64:128, :].rearrange("p (h s) -> p h s", h=H)[:, :, sp:sp + 32]
        kdst = KA[64:128, :].rearrange("p (h s) -> p h s", h=H)[:, :, sp:sp + 32]
        vdst = VA[0:64, :].rearrange("p (h s) -> p h s", h=H)[:, :, sp:sp + 32]
        qsrc = b0[64:128, :].rearrange("p (h i) -> p h i", h=H)
        ksrc = b1[64:128, :].rearrange("p (h i) -> p h i", h=H)
        vsrc = b0[0:64, :].rearrange("p (h i) -> p h i", h=H)
        p.v.wait(p.t)
        p.s.wait(p.t)
        p.v.mark(lambda v, o=qdst, i=qsrc: v.tensor_copy(out=o, in_=i))
        p.s.mark(lambda s, o=kdst, i=ksrc: s.copy(out=o, in_=i))
        p.v.mark(lambda v, o=vdst, i=vsrc: v.tensor_copy(out=o, in_=i))
        ev_marks[sl] = [(p.v, p.v.n), (p.s, p.s.n)]
    p.t.wait(p.v)
    p.t.wait(p.s)
    proj_done_v = p.v.n
    proj_done_s = p.s.n

    RF = VA[64:128, :]
    EH = T["ESC"][64:128, :]
    sh = seq * H
    zpart = T["rows"][64:128, 0:16]
    rsum = T["rows"][64:128, 16:32]
    zinc = T["rows"][64:128, 32:48]
    gz = T["rows"][0:1, 512:1536]

    def sqsk_quarter(q):
        blk = slice(q * 2048, (q + 1) * 2048)
        rfq = RF[:, blk]
        ehq = EH
        p.s.wait(p.v)
        p.s.mark(lambda s, rfq=rfq: s.activation(out=rfq, in_=QA[64:128, blk], func=AF.Relu))
        p.v.wait(p.s)
        p.v.mark(lambda v: v.tensor_sub(out=SQA[64:128, blk], in0=QA[64:128, blk], in1=rfq))
        p.s.wait(p.v)
        p.s.mark(lambda s: s.activation(out=ehq, in_=SQA[64:128, blk], func=AF.Exp,
                                        scale=1.0 / QSCALE))
        p.v.wait(p.s)
        p.v.mark(lambda v: v.scalar_tensor_tensor(
            out=SQA[64:128, blk], in0=rfq, scalar=1.0 / QSCALE, in1=ehq,
            op0=ALU.mult, op1=ALU.add))
        p.state[("sqq", q)] = p.v.n
        p.s.wait(p.v)
        p.s.mark(lambda s, rfq=rfq: s.activation(out=rfq, in_=KA[64:128, blk], func=AF.Relu))
        p.v.wait(p.s)
        p.v.emit(lambda v, o=rsum[:, q * 4:(q + 1) * 4], rfq=rfq: v.reduce_sum(
            out=o, in_=rfq.rearrange("p (h l) -> p h l", h=4), axis=AX.X))
        p.v.mark(lambda v: v.tensor_sub(out=ehq, in0=KA[64:128, blk], in1=rfq))
        p.s.wait(p.v)
        p.s.mark(lambda s, rfq=rfq: s.activation(out=rfq, in_=ehq, func=AF.Exp))
        p.v.wait(p.s)
        p.v.emit(lambda v, o=zpart[:, q * 4:(q + 1) * 4], rfq=rfq: v.reduce_sum(
            out=o, in_=rfq.rearrange("p (h l) -> p h l", h=4), axis=AX.X))
        p.v.emit(lambda v, o=zinc[:, q * 4:(q + 1) * 4]: v.scalar_tensor_tensor(
            out=o, in0=rsum[:, q * 4:(q + 1) * 4], scalar=1.0,
            in1=zpart[:, q * 4:(q + 1) * 4], op0=ALU.mult, op1=ALU.add))
        zcs = T["zc_sb"][64:128, sh + q * 4:sh + (q + 1) * 4]
        p.v.mark(lambda v, zcs=zcs: v.tensor_add(out=zcs, in0=zcs,
                                                 in1=zinc[:, q * 4:(q + 1) * 4]))
        zq_ready = p.v.n
        p.y.wait(p.v, zq_ready)
        st1 = p.dma("y", "zc", T["s_zc"][:, q * 4:(q + 1) * 4],
                    T["zc_sb"][64:128, sh + q * 4:sh + (q + 1) * 4])
        p.y.wait(p.stream("zc"), st1)
        gq = slice(q * 256, (q + 1) * 256)
        st2 = p.dma("y", "zc", rows[0:1, 512 + q * 256:512 + (q + 1) * 256],
                    T["s_zc"][:, q * 4:(q + 1) * 4].transpose([1, 0]),
                    allow_slow_non_contiguous=True)
        p.v.wait(p.stream("zc"), st2)
        p.v.emit(lambda v, gq=gq: v.reciprocal(out=gz[:, gq], in_=gz[:, gq]))
        p.v.mark(lambda v, gq=gq: v.tensor_mul(out=gz[:, gq], in0=T["grow"][:, gq],
                                               in1=gz[:, gq]))
        p.state[("gzq", q)] = p.v.n

    AV_BANK, NUM_BANK = 5, 6
    A_S_BANKS = (0, 1)
    PTB = T["P8"][:, 2 * 512:4 * 512].bitcast(BF16)
    TRB = T["P8"][:, 4 * 512:5 * 512].bitcast(BF16)
    B7 = BK[7]
    VT = T["VTST"][:, 0:256]
    ST = T["VTST"][:, 256:512]
    rows = T["rows"]
    cols = T["cols"]

    def small_pe_1(h):
        p.t.wait(p.v, p.state[("sqq", h // 4)])
        p.t.wait(p.v, p.state.get(("a7",), 0))
        p.t.wait(p.s, p.state.get(("a7s",), 0))
        p.t.mark(lambda t, i=SQA[64:128, h * 512:(h + 1) * 512]: t.matmul(
            B7[0:1, 0:512], T["onesc"][64:128, 0:1], i, start=True, stop=True,
            tile_position=(64, 0)))
        p.v.wait(p.t)
        p.v.mark(lambda v: v.reciprocal(out=rows[0:1, 0:512], in_=B7[0:1, 0:512]))
        p.state[("a7",)] = p.v.n

    def small_pe_2(h):
        p.t.wait(p.v, p.state[("a7",)])
        p.t.wait(p.s, p.state.get(("a7s",), 0))
        p.t.wait(p.v, p.state[("gzq", h // 4)])

        def t_b(t, h=h):
            last = None
            for pp in range(2):
                last = t.matmul(B7[pp * 64:pp * 64 + 64, 0:256],
                                rows[0:1, 512 + h * 64:512 + (h + 1) * 64],
                                rows[0:1, pp * 256:pp * 256 + 256],
                                start=True, stop=True, tile_position=(0, pp * 64))
            return last

        p.t.mark(t_b)
        p.s.wait(p.t)
        p.s.mark(lambda s, o=T["BSB"][:, (h % 2) * 256:(h % 2) * 256 + 256]:
                 s.copy(out=o, in_=B7[:, 0:256]))
        p.state[("a7s",)] = p.s.n

    def scores_part(h):
        hof = h * 512
        cb = (h % 2) * 8
        for sc in range(4):
            sb_ = A_S_BANKS[sc % 2]
            p.t.wait(p.s, p.state.get(("aS_free", sb_), 0))
            p.t.mark(lambda t, b=BK[sb_], lh=QA[64:128, hof + sc * 128:hof + sc * 128 + 128],
                     rh=KA[64:128, hof:hof + 512]: t.matmul(b, lh, rh, start=True, stop=True))
            p.s.wait(p.t)
            p.s.wait(p.t, p.state.get(("aPBt", h % 2), 0))
            pb = PB[:, ((h % 2) * 4 + sc) * 512:((h % 2) * 4 + sc + 1) * 512]
            p.s.mark(lambda s, o=pb, b=BK[sb_], c2=cols[:, cb + sc:cb + sc + 1]:
                     s.activation(out=o, in_=b, func=AF.Exp, bias=T["c12col"], accum_out=c2))
            p.state[("aS_free", sb_)] = p.s.n
        p.v.wait(p.s)
        p.v.mark(lambda v: v.reciprocal(out=cols[:, cb + 4:cb + 8], in_=cols[:, cb:cb + 4]))
        for sc in range(4):
            pb = PB[:, ((h % 2) * 4 + sc) * 512:((h % 2) * 4 + sc + 1) * 512]
            p.v.mark(lambda v, o=pb, c3=cols[:, cb + 4 + sc:cb + 5 + sc]: v.tensor_scalar(
                out=o, in0=o, scalar1=c3, scalar2=None, op0=ALU.mult))
        p.state[("norms", h)] = p.v.n

    def tail_part(h):
        hof = h * 512
        p.t.wait(p.s, p.state.get(("aTR_free",), 0))

        def t_vt4(t, hof=hof):
            last = None
            for j in range(4):
                last = t.transpose(TRB[:, j * 64:(j + 1) * 64],
                                   VA[0:64, hof + j * 128:hof + (j + 1) * 128],
                                   T["ident64a"][0:64, :])
            return last

        def t_st4(t, hof=hof):
            last = None
            for j in range(4):
                last = t.transpose(TRB[:, 256 + j * 64: 256 + (j + 1) * 64],
                                   SQA[64:128, hof + j * 128:hof + (j + 1) * 128],
                                   T["ident64"][64:128, :])
            return last

        m1 = p.t.mark(t_vt4)
        p.t.wait(p.t, m1)
        p.t.mark(t_st4)
        p.s.wait(p.t)
        p.s.mark(lambda s: s.copy(out=T["VTST"], in_=TRB[:, 0:512]))
        vtst_ready = p.s.n
        p.state[("aTR_free",)] = p.s.n
        p.t.wait(p.v, p.state[("norms", h)])
        p.t.wait(p.v, p.state.get(("aAV_free",), 0))
        evac_marks = {}
        for sc in range(5):
            if sc < 4:
                tb = PTB[:, (sc % 2) * 1024:(sc % 2) * 1024 + 512]
                p.t.wait(p.s, p.state.get(("aPT_free", sc % 2), 0))

                def t_pt4(t, h=h, sc=sc, tb=tb):
                    last = None
                    for j in range(4):
                        last = t.transpose(tb[:, j * 128:(j + 1) * 128],
                                           PB[:, ((h % 2) * 4 + sc) * 512 + j * 128:
                                              ((h % 2) * 4 + sc) * 512 + (j + 1) * 128],
                                           T["ident"])
                    return last

                p.t.mark(t_pt4)
                p.s.wait(p.t)
                out = PT.rearrange("p (j s2) -> p j s2", j=4)[:, :, sc * 128:(sc + 1) * 128]
                p.s.mark(lambda s, out=out, tb=tb: s.copy(
                    out=out, in_=tb.rearrange("p (j c) -> p j c", j=4)))
                p.state[("aPT_free", sc % 2)] = p.s.n
                evac_marks[sc] = p.s.n
            if sc >= 1:
                sp_ = sc - 1
                pp = sp_ // 2
                pq = 1 - pp
                p.t.wait(p.s, evac_marks[sp_])

                def t_av4(t, sp_=sp_, pp=pp):
                    last = None
                    for j in range(4):
                        last = t.matmul(BK[AV_BANK][pp * 64:pp * 64 + 64, sp_ * 128:(sp_ + 1) * 128],
                                        VT[:, j * 64:(j + 1) * 64],
                                        PT[:, j * 512 + sp_ * 128: j * 512 + (sp_ + 1) * 128],
                                        start=(j == 0), stop=(j == 3), tile_position=(0, pp * 64))
                    return last

                p.t.mark(t_av4)
        p.state[("aPBt", h % 2)] = p.t.n
        p.t.wait(p.s, vtst_ready)
        p.t.wait(p.v, p.state.get(("a7",), 0))
        p.t.wait(p.s, p.state.get(("a7s",), 0))

        def t_mem(t):
            last = None
            for j in range(4):
                last = t.matmul(B7[64:128, 256:320], ST[:, j * 64:(j + 1) * 64],
                                VT[:, j * 64:(j + 1) * 64],
                                start=(j == 0), stop=(j == 3), tile_position=(0, 64))
            return last

        p.t.mark(t_mem)
        p.v.wait(p.t)
        mm = T["mem_sb"][64:128, (sh + h) * 64:(sh + h + 1) * 64]
        p.v.mark(lambda v, mm=mm: v.tensor_add(out=mm, in0=mm, in1=B7[64:128, 256:320]))
        p.v.mark(lambda v, mm=mm: v.tensor_copy(out=T["membf"][64:128, :], in_=mm))
        mem_ready = p.v.n
        p.state[("a7",)] = p.v.n
        p.t.wait(p.v, mem_ready)
        p.t.wait(p.v, p.state.get(("aNUM_free",), 0))

        def t_num(t, hof=hof):
            last = None
            for sc in (0, 2, 1, 3):
                pp = sc // 2
                last = t.matmul(BK[NUM_BANK][pp * 64:pp * 64 + 64, sc * 128:(sc + 1) * 128],
                                T["membf"][64:128, :],
                                SQA[64:128, hof + sc * 128:hof + (sc + 1) * 128],
                                start=True, stop=True, tile_position=(64, pp * 64))
            return last

        p.t.mark(t_num)
        p.v.wait(p.t)
        p.v.wait(p.s, p.state.get(("a7s",), 0))
        for pp in range(2):
            tsb = T["TSB"][pp * 64:pp * 64 + 64, :]
            p.v.emit(lambda v, o=tsb, nu=BK[NUM_BANK][pp * 64:pp * 64 + 64, pp * 256:pp * 256 + 256],
                     bs=T["BSB"][pp * 64:pp * 64 + 64, (h % 2) * 256:(h % 2) * 256 + 256]:
                     v.scalar_tensor_tensor(out=o, in0=nu, scalar=1.0, in1=bs,
                                            op0=ALU.mult, op1=ALU.mult))
            a2o = A2[pp * 64:pp * 64 + 64, :].rearrange("p (cc l) -> p cc l", cc=8)[:, :, h * 32:(h + 1) * 32]
            p.v.mark(lambda v, o=a2o, av=BK[AV_BANK][pp * 64:pp * 64 + 64, pp * 256:pp * 256 + 256],
                     ig=T["invg"][pp * 64:pp * 64 + 64, h:h + 1], ts=tsb:
                     v.scalar_tensor_tensor(out=o, in0=av, scalar=ig, in1=ts,
                                            op0=ALU.mult, op1=ALU.add))
        p.state[("aAV_free",)] = p.v.n
        p.state[("aNUM_free",)] = p.v.n

    for h in range(H if _AA_LIMIT >= 1 else 0):
        if h % 4 == 0:
            sqsk_quarter(h // 4)
        small_pe_1(h)
        scores_part(h)
        if h >= 1:
            tail_part(h - 1)
        small_pe_2(h)
    if _AA_LIMIT >= 1 and H > 0:
        tail_part(H - 1)
    a2_ready = p.v.n

    for g in range(16 if _AA_LIMIT >= 7 else 0):
        sl = g % 2
        pp = g % 2
        cc = g // 2
        if g >= 2:
            p.y.wait(p.t, p.state[("awo_pe", sl)])
        dst = wo_buf[pp * 64:pp * 64 + 64, sl * D:(sl + 1) * D]
        ld = p.dma("y", ("wo", sl), dst, T["s_wo"][g])
        p.t.wait(p.stream(("wo", sl)), ld)
        if g == 0:
            p.t.wait(p.v, a2_ready)

        def t_wo(t, g=g, sl=sl, pp=pp, cc=cc):
            last = None
            for lc in range(4):
                for dhf in range(2):
                    last = t.matmul(BK[lc * 2 + dhf],
                                    A2[pp * 64:pp * 64 + 64, cc * 512 + lc * 128: cc * 512 + (lc + 1) * 128],
                                    wo_buf[pp * 64:pp * 64 + 64, sl * D + dhf * 512: sl * D + (dhf + 1) * 512],
                                    start=(g == 0), stop=(g == 15), tile_position=(pp * 64, 0))
            return last

        p.t.mark(t_wo)
        p.state[("awo_pe", sl)] = p.t.n
    if _AA_LIMIT >= 7:
        p.v.wait(p.t)
        for lc in range(4):
            xs = T["X"][:, seq * 4096 + lc * D: seq * 4096 + (lc + 1) * D]
            b = T["P8"][:, lc * 1024:(lc + 1) * 1024]
            p.v.mark(lambda v, xs=xs, b=b: v.tensor_add(out=xs, in0=xs, in1=b))
        p.state[("X", seq)] = (p.v, p.v.n)
    p.state["xlf_free"] = [(p.t, p.t.n)]
    p.state["banks_free"] = (p.v, p.v.n)


def ffn(nc, p, T):
    BK = T["BK"]
    wait_scratch(p, p.y)
    p.y.wait(p.t)
    p.t.wait(p.g)
    p.v.wait(p.g)
    ln_into_xlf(nc, p, T, seq=0, xlf=T["xlf"])
    xlf_a_mark = p.state["xlf"]
    ln_into_xlf(nc, p, T, seq=1, xlf=T["xlf2"])
    xlf_b_mark = p.state["xlf"]
    p.t.wait(*xlf_a_mark)
    p.t.wait(*xlf_b_mark)
    q, val = p.state.get("banks_free", (None, 0))
    if q is not None:
        p.t.wait(q, val)
    xls = (T["xlf"], T["xlf2"])
    hts = (T["HA"], T["HB"])
    for c in range(24):
        sl = c % 3
        if c >= 3:
            p.y.wait(p.t, p.state[("f1_pe", sl)])
        for idx, s_w in enumerate((T["s_wg"], T["s_wu"])):
            dst = T["wbuf_1"][:, sl * 2048 + idx * 1024: sl * 2048 + (idx + 1) * 1024]
            ld = p.dma("y", ("f1", sl), dst.rearrange("p (k j) -> p k j", k=8),
                       s_w[c].transpose([1, 0, 2]))
        p.t.wait(p.stream(("f1", sl)), ld)
        for sq in range(NSEQ):
            bG = BK[(sl * 2 + sq) % 4]
            bU = BK[4 + (sl * 2 + sq) % 4]
            key = ("f1_banks", (sl * 2 + sq) % 4)
            qv = p.state.get(key, 0)
            p.t.wait(p.v, qv)

            def t_gu(t, sl=sl, sq=sq, bG=bG, bU=bU):
                last = None
                for k in range(8):
                    last = t.matmul(bG, T["wbuf_1"][:, sl * 2048 + k * 128:(sl * 2048 + (k + 1) * 128)],
                                    xls[sq][:, k * 512:(k + 1) * 512],
                                    start=(k == 0), stop=(k == 7))
                for k in range(8):
                    last = t.matmul(bU, T["wbuf_1"][:, sl * 2048 + 1024 + k * 128: sl * 2048 + 1024 + (k + 1) * 128],
                                    xls[sq][:, k * 512:(k + 1) * 512],
                                    start=(k == 0), stop=(k == 7))
                return last

            p.t.mark(t_gu)
            p.state[("f1_pe", sl)] = p.t.n
            ssl = (c * 2 + sq) % 2
            ssb = T["SSB"][:, ssl * 512:(ssl + 1) * 512]
            p.s.wait(p.t)
            p.s.wait(p.v, p.state.get(("ssb_free", ssl), 0))
            p.s.mark(lambda s, bG=bG, ssb=ssb: s.activation(out=ssb, in_=bG, func=AF.Silu))
            p.v.wait(p.s)
            hdst = hts[sq][:, c * 512:(c + 1) * 512]
            p.v.mark(lambda v, o=hdst, bU=bU, ssb=ssb: v.tensor_mul(out=o, in0=ssb, in1=bU))
            p.state[key] = p.v.n
            p.state[("ssb_free", ssl)] = p.v.n
    h_ready = p.v.n
    for sq in range(NSEQ):
        for c in range(24):
            sl = c % 3
            if c >= 3 or sq > 0:
                key = ("f2_pe", sl)
                if key in p.state:
                    p.y.wait(p.t, p.state[key])
            dst = T["wbuf_2"][:, sl * 1024:(sl + 1) * 1024]
            ld = p.dma("y", ("f2", sl), dst.rearrange("p (k j) -> p k j", k=8),
                       T["s_wd"][c].transpose([1, 0, 2]))
            p.t.wait(p.stream(("f2", sl)), ld)
            if c == 0:
                p.t.wait(p.v, h_ready if sq == 0 else p.state[("f2_res", 0)])

            def t_dn(t, sq=sq, c=c, sl=sl):
                last = None
                for lc in range(4):
                    for dhf in range(2):
                        last = t.matmul(BK[lc * 2 + dhf],
                                        hts[sq][:, c * 512 + lc * 128: c * 512 + (lc + 1) * 128],
                                        T["wbuf_2"][:, sl * 1024 + dhf * 512: sl * 1024 + (dhf + 1) * 512],
                                        start=(c == 0), stop=(c == 23))
                return last

            p.t.mark(t_dn)
            p.state[("f2_pe", sl)] = p.t.n
        p.v.wait(p.t)
        for lc in range(4):
            xs = T["X"][:, sq * 4096 + lc * D: sq * 4096 + (lc + 1) * D]
            b = T["P8"][:, lc * 1024:(lc + 1) * 1024]
            p.v.mark(lambda v, xs=xs, b=b: v.tensor_add(out=xs, in0=xs, in1=b))
        p.state[("f2_res", sq)] = p.v.n
        p.state[("X", sq)] = (p.v, p.v.n)
    p.state["xlf_free"] = [(p.t, p.t.n)]
    p.state["banks_free"] = (p.v, p.v.n)


def attn_b(nc, p, T, seq, n):
    BK = T["BK"]
    QKVB, ABT, PB2, PT2, VT2 = T["QKVB"], T["ABT"], T["PB2"], T["PT2"], T["VT2"]
    cols = T["cols"]
    wait_scratch(p, p.y)
    p.y.wait(p.t)
    p.t.wait(p.s)
    p.v.wait(p.s)
    xq, xv = p.state["xlf"]
    p.t.wait(xq, xv)
    q, val = p.state.get("banks_free", (None, 0))
    if q is not None:
        p.t.wait(q, val)
    for c in range(24):
        sl = c % 3
        if c >= 3:
            p.y.wait(p.t, p.state[("bproj_pe", sl)])
        dst = T["wbuf_1"][:, sl * 2048: sl * 2048 + 1024]
        ld = p.dma("y", ("qkv", sl), dst.rearrange("p (k j) -> p k j", k=8),
                   T["s_wqkv"][c].transpose([1, 0, 2]))
        p.t.wait(p.stream(("qkv", sl)), ld)
        bank = BK[c % 2]
        qv = p.state.get(("bproj_banks", c % 2), 0)
        p.t.wait(p.v, qv)
        p.t.wait(p.s, p.state.get(("bproj_banks_s", c % 2), 0))

        def t_p(t, sl=sl, bank=bank):
            last = None
            for k in range(8):
                last = t.matmul(bank, T["wbuf_1"][:, sl * 2048 + k * 128: sl * 2048 + (k + 1) * 128],
                                T["xlf"][:, k * 512:(k + 1) * 512],
                                start=(k == 0), stop=(k == 7))
            return last

        p.t.mark(t_p)
        p.state[("bproj_pe", sl)] = p.t.n
        dstq = QKVB[:, c * 512:(c + 1) * 512]
        if c % 2 == 0:
            p.v.wait(p.t)
            p.v.mark(lambda v, o=dstq, b=bank: v.tensor_copy(out=o, in_=b))
            p.state[("bproj_banks", c % 2)] = p.v.n
        else:
            p.s.wait(p.t)
            p.s.mark(lambda s, o=dstq, b=bank: s.copy(out=o, in_=b))
            p.state[("bproj_banks_s", c % 2)] = p.s.n
    p.t.wait(p.v)
    p.t.wait(p.s)
    qkv_ready_v, qkv_ready_s = p.v.n, p.s.n

    S_BANKS = (0, 1, 6)
    AV_BANK = 5
    PTB2 = T["P8"][:, 2 * 512:4 * 512].bitcast(BF16)
    TRB2 = T["P8"][:, 4 * 512:5 * 512].bitcast(BF16)

    def scores_b(h):
        b = (h % 2) * 64
        cq, ck, cv = h // 2, 8 + h // 2, 16 + h // 2
        idt = T["ident64a"][0:64, :] if b == 0 else T["ident64"][64:128, :]
        cb = 16 + (h % 2) * 8
        p.t.wait(p.s, p.state.get(("bTR_free",), 0))

        def t_vt(t, b=b, cv=cv, idt=idt):
            last = None
            for j in range(2):
                src = T["vwin"][b:b + 64, seq * 2048 + (cv - 16) * 256 + j * 128:
                                seq * 2048 + (cv - 16) * 256 + (j + 1) * 128]
                last = t.transpose(TRB2[:, j * 64:(j + 1) * 64], src, idt)
            for j in range(4):
                src = QKVB[b:b + 64, cv * 512 + j * 128: cv * 512 + (j + 1) * 128]
                last = t.transpose(TRB2[:, (2 + j) * 64:(3 + j) * 64], src, idt)
            return last

        p.t.mark(t_vt)
        p.s.wait(p.t)
        p.s.mark(lambda s, o=VT2[:, (h % 2) * 384:(h % 2) * 384 + 384]:
                 s.copy(out=o, in_=TRB2[:, 0:384]))
        p.state[("vt", h)] = p.s.n
        p.state[("bTR_free",)] = p.s.n
        for qc in range(4):
            sb_ = S_BANKS[(h * 4 + qc) % 3]
            lh = QKVB[b:b + 64, cq * 512 + qc * 128: cq * 512 + (qc + 1) * 128]
            pieces = []
            wbase = seq * 2048 + (ck - 8) * 256
            if qc == 0:
                pieces.append((0, 256, T["kwin"][b:b + 64, wbase:wbase + 256]))
                pieces.append((256, 384, QKVB[b:b + 64, ck * 512: ck * 512 + 128]))
            elif qc == 1:
                pieces.append((0, 128, T["kwin"][b:b + 64, wbase + 128:wbase + 256]))
                pieces.append((128, 384, QKVB[b:b + 64, ck * 512: ck * 512 + 256]))
            elif qc == 2:
                pieces.append((0, 384, QKVB[b:b + 64, ck * 512: ck * 512 + 384]))
            else:
                pieces.append((0, 384, QKVB[b:b + 64, ck * 512 + 128: ck * 512 + 512]))
            p.t.wait(p.s, p.state.get(("bS_free", sb_), 0))
            p.t.wait(p.v, p.state.get(("bS_freev", sb_), 0))

            def t_sc(t, lh=lh, pieces=pieces, sb_=sb_):
                last = None
                for (c0, c1, src) in pieces:
                    last = t.matmul(BK[sb_][:, c0:c1], lh, src, start=True, stop=True)
                return last

            p.t.mark(t_sc)
            if n == 0 and qc == 0:
                msk = T["mask_f0"]
            elif n == 0 and qc == 1:
                msk = T["mask_f1"]
            else:
                msk = T["mask_std"]
            p.v.wait(p.t)
            p.v.mark(lambda v, b_=BK[sb_], m=msk: v.tensor_add(out=b_[:, 0:384],
                                                               in0=b_[:, 0:384], in1=m))
            p.state[("bS_freev", sb_)] = p.v.n
            p.s.wait(p.v)
            p.s.wait(p.t, p.state.get(("bPBt", h % 2), 0))
            pb = PB2[:, ((h % 2) * 4 + qc) * 384:((h % 2) * 4 + qc + 1) * 384]
            c2 = cols[:, cb + qc:cb + qc + 1]
            p.s.mark(lambda s, o=pb, b_=BK[sb_], c2=c2: s.activation(
                out=o, in_=b_[:, 0:384], func=AF.Exp, accum_out=c2))
            p.state[("bS_free", sb_)] = p.s.n
        p.v.wait(p.s)
        p.v.mark(lambda v: v.reciprocal(out=cols[:, cb + 4:cb + 8], in_=cols[:, cb:cb + 4]))
        for qc in range(4):
            pb = PB2[:, ((h % 2) * 4 + qc) * 384:((h % 2) * 4 + qc + 1) * 384]
            p.v.mark(lambda v, o=pb, c3=cols[:, cb + 4 + qc:cb + 5 + qc]: v.tensor_scalar(
                out=o, in0=o, scalar1=c3, scalar2=None, op0=ALU.mult))
        p.state[("bnorms", h)] = p.v.n

    def tail_b(h):
        b = (h % 2) * 64
        cq = h // 2
        p.t.wait(p.v, p.state[("bnorms", h)])
        p.t.wait(p.s, p.state[("vt", h)])
        p.t.wait(p.s, p.state.get(("bAV_free",), 0))
        evac_marks = {}
        for qc in range(5):
            if qc < 4:
                tb = PTB2[:, (qc % 2) * 1024:(qc % 2) * 1024 + 384]
                p.t.wait(p.s, p.state.get(("bPT_free", qc % 2), 0))

                def t_pt3(t, h=h, qc=qc, tb=tb):
                    last = None
                    for j in range(3):
                        last = t.transpose(tb[:, j * 128:(j + 1) * 128],
                                           PB2[:, ((h % 2) * 4 + qc) * 384 + j * 128:
                                               ((h % 2) * 4 + qc) * 384 + (j + 1) * 128],
                                           T["ident"])
                    return last

                p.t.mark(t_pt3)
                p.s.wait(p.t)
                p.s.mark(lambda s, o=PT2[:, qc * 384:(qc + 1) * 384], tb=tb:
                         s.copy(out=o, in_=tb))
                p.state[("bPT_free", qc % 2)] = p.s.n
                evac_marks[qc] = p.s.n
            if qc >= 1:
                qp = qc - 1
                p.t.wait(p.s, evac_marks[qp])

                def t_av3(t, b=b, h=h, qp=qp):
                    last = None
                    for j in range(3):
                        kc = qp + j
                        last = t.matmul(BK[AV_BANK][b:b + 64, qp * 128:(qp + 1) * 128],
                                        VT2[:, (h % 2) * 384 + kc * 64:(h % 2) * 384 + (kc + 1) * 64],
                                        PT2[:, qp * 384 + j * 128: qp * 384 + (j + 1) * 128],
                                        start=(j == 0), stop=(j == 2), tile_position=(0, b))
                    return last

                p.t.mark(t_av3)
        p.state[("bPBt", h % 2)] = p.t.n
        p.s.wait(p.t)
        abdst = ABT[b:b + 64, :].rearrange("p (cc l) -> p cc l", cc=8)[:, cq:cq + 1, :]
        avsrc = BK[AV_BANK][b:b + 64, 0:512].rearrange("p (cc l) -> p cc l", cc=1)
        p.s.mark(lambda s, o=abdst, i=avsrc: s.copy(out=o, in_=i))
        p.state[("bAV_free",)] = p.s.n

    for h in range(H):
        scores_b(h)
        if h >= 1:
            tail_b(h - 1)
    tail_b(H - 1)
    ab_ready_s = p.s.n
    p.s.wait(p.t)
    for c in range(8):
        ksrc = QKVB[:, (8 + c) * 512 + 256:(8 + c) * 512 + 512]
        vsrc = QKVB[:, (16 + c) * 512 + 256:(16 + c) * 512 + 512]
        kdst = T["kwin"][:, seq * 2048 + c * 256: seq * 2048 + (c + 1) * 256]
        vdst = T["vwin"][:, seq * 2048 + c * 256: seq * 2048 + (c + 1) * 256]
        p.s.emit(lambda s, o=kdst, i=ksrc: s.copy(out=o, in_=i))
        p.s.mark(lambda s, o=vdst, i=vsrc: s.copy(out=o, in_=i))
    p.state["win_update"] = (p.s, p.s.n)
    for c in range(8):
        sl = c % 3
        if c >= 3:
            p.y.wait(p.t, p.state[("bwo_pe", sl)])
        dst = T["wbuf_2"][:, sl * 1024:(sl + 1) * 1024]
        ld = p.dma("y", ("wout", sl), dst, T["s_wout"][c])
        p.t.wait(p.stream(("wout", sl)), ld)
        if c == 0:
            p.t.wait(p.s, ab_ready_s)
            p.t.wait(p.v)

        def t_wo(t, c=c, sl=sl):
            last = None
            for lc in range(4):
                for dhf in range(2):
                    last = t.matmul(BK[lc * 2 + dhf],
                                    ABT[:, c * 512 + lc * 128: c * 512 + (lc + 1) * 128],
                                    T["wbuf_2"][:, sl * 1024 + dhf * 512: sl * 1024 + (dhf + 1) * 512],
                                    start=(c == 0), stop=(c == 7))
            return last

        p.t.mark(t_wo)
        p.state[("bwo_pe", sl)] = p.t.n
    p.v.wait(p.t)
    for lc in range(4):
        xs = T["X"][:, seq * 4096 + lc * D: seq * 4096 + (lc + 1) * D]
        b = T["P8"][:, lc * 1024:(lc + 1) * 1024]
        p.v.mark(lambda v, xs=xs, b=b: v.tensor_add(out=xs, in0=xs, in1=b))
    p.state[("X", seq)] = (p.v, p.v.n)
    p.state["xlf_free"] = [(p.t, p.t.n)]
    p.state["banks_free"] = (p.v, p.v.n)


def store_segment(nc, p, T, seq, n):
    xq, xv = p.state[("X", seq)]
    p.g.wait(xq, xv)
    xs = T["X"][:, seq * 4096:(seq + 1) * 4096].rearrange("p (lc d) -> p lc d", lc=4)
    dst = T["yout"][seq, n * SEG:(n + 1) * SEG, :].rearrange("(lc q) d -> q lc d", q=128)
    st = p.dma("g", ("x", seq), dst, xs)
    p.state[("Xstore", seq)] = st
    if n + 1 < _NSEG_RUN:
        p.g.wait(p.stream(("x", seq)), st)
        src2 = T["xin"][seq, (n + 1) * SEG:(n + 2) * SEG, :].rearrange("(lc q) d -> q lc d", q=128)
        ld = p.dma("g", ("x", seq), xs, src2)
        p.state[("X", seq)] = (p.stream(("x", seq)), ld)


def epilogue(nc, p, T):
    for key, ctr in p.dstream.items():
        p.g.wait(ctr, ctr.n)



def kernel(L0_attn_a_x, L0_attn_a_Wq, L0_attn_a_Wk, L0_attn_a_Wv, L0_attn_a_Wo,
           L0_attn_a_betas, L0_attn_b_w_qkv, L0_attn_b_w_out, L0_ffn_w_gate,
           L0_ffn_w_up, L0_ffn_w_down):
    x = np.ascontiguousarray(L0_attn_a_x)
    nc = build_program()
    in_maps = []
    for c in range(8):
        in_maps.append({
            "x": np.ascontiguousarray(x[2 * c:2 * c + 2]),
            "wq": np.ascontiguousarray(L0_attn_a_Wq),
            "wk": np.ascontiguousarray(L0_attn_a_Wk),
            "wv": np.ascontiguousarray(L0_attn_a_Wv),
            "wo": np.ascontiguousarray(L0_attn_a_Wo),
            "betas": np.ascontiguousarray(L0_attn_a_betas),
            "wqkv": np.ascontiguousarray(L0_attn_b_w_qkv),
            "wout": np.ascontiguousarray(L0_attn_b_w_out),
            "wg": np.ascontiguousarray(L0_ffn_w_gate),
            "wu": np.ascontiguousarray(L0_ffn_w_up),
            "wd": np.ascontiguousarray(L0_ffn_w_down),
        })
    res = run_bass_kernel_spmd(nc, in_maps, core_ids=list(range(8)))
    return np.concatenate([r["y"] for r in res.results], axis=0)
